# version 18
# baseline (speedup 1.0000x reference)
import numpy as np
import concourse.bass as bass
import concourse.mybir as mybir
from concourse.bass_utils import run_bass_kernel_spmd
from concourse.alu_op_type import AluOpType as Op

F32 = mybir.dt.float32
BF16 = mybir.dt.bfloat16
AF = mybir.ActivationFunctionType
AX = mybir.AxisListType.X

LN2 = float(np.log(2.0))
LN2_INV = float(1.0 / np.log(2.0))
M32 = float(3 * 2.0 ** 22)
KBIG = float(3 * 2.0 ** 81)
EPS_FLAG = float(2.0 ** -30)

NCORES = 8
D = 1024
HID = 4096
DC = D // 128
HC = HID // 128

ENGINES = ("sync", "gpsimd", "vector", "scalar", "tensor")
ALLSEMS = (["dls", "dlg", "dout", "cc", "sv", "ss", "sg", "pe"] +
           [f"ds{i}" for i in range(4)] + [f"dv{i}" for i in range(4)] +
           [f"dq{i}" for i in range(4)] + ["dx0", "dx1", "dw0", "dw1"])


class Prog:
    def __init__(self):
        self.ops = {e: [] for e in ENGINES}
        self.count = {s: 0 for s in ALLSEMS}
        self.waited = {e: {} for e in ENGINES}

    DMA_SEMS = tuple(["dls", "dlg", "dout"] + [f"ds{i}" for i in range(4)] +
                     [f"dv{i}" for i in range(4)] + [f"dq{i}" for i in range(4)] +
                     ["dx0", "dx1", "dw0", "dw1"])

    def op(self, eng, fn, deps=(), inc=None, amt=1):
        waits = []
        for d in deps:
            if d is None:
                continue
            s, v = d
            if s in self.DMA_SEMS:
                v = self.count[s]
            if v > self.waited[eng].get(s, 0):
                waits.append((s, v))
                self.waited[eng][s] = v
        tok = None
        if inc is not None:
            self.count[inc] += amt
            tok = (inc, self.count[inc])
        self.ops[eng].append((waits, fn, inc, amt))
        return tok

    def dma(self, eng, fn, sem, deps=()):
        return self.op(eng, fn, deps, inc=sem, amt=16)

    def barrier(self):
        return [(s, self.count[s]) for s in
                ("sv", "ss", "sg", "pe") + self.DMA_SEMS]

    def replay(self, block, sems):
        def make(eng_name):
            def body(e):
                for waits, fn, inc, amt in self.ops[eng_name]:
                    for s, v in waits:
                        e.wait_ge(sems[s], v)
                    inst = fn(e)
                    if inc is not None:
                        assert inst is not None, f"inc={inc} op returned no inst"
                        inst.then_inc(sems[inc], amt)
            return body
        for e in ENGINES:
            if self.ops[e]:
                getattr(block, e)(make(e))


class Layout:
    def __init__(self, ap_f32, n_f32):
        self.ap = ap_f32
        self.apbf = ap_f32.bitcast(BF16)
        self.n = n_f32
        self.off = 0

    def reset(self):
        self.off = 0

    def f32(self, n):
        o = self.off
        assert o + n <= self.n, f"slab overflow f32 {o}+{n}>{self.n}"
        self.off = o + n
        return self.ap[:, o:o + n]

    def bf16(self, n):
        n_f = (n + 1) // 2
        o = self.off
        assert o + n_f <= self.n, f"slab overflow bf16 {o}+{n_f}>{self.n}"
        self.off = o + n_f
        return self.apbf[:, 2 * o:2 * o + n]


def emit_magic(p, t):
    m1 = p.op("vector", lambda e: e.memset(t["ones"][:], 1.0), inc="sv")
    m2 = p.op("vector", lambda e: e.memset(t["kb"][:], KBIG), inc="sv")
    a = p.op("vector", lambda e: e.tensor_scalar(
        t["flagc"][:], t["ones"][:], EPS_FLAG, 1.0, op0=Op.add, op1=Op.subtract),
        deps=[m1, m2], inc="sv")
    b = p.op("vector", lambda e: e.tensor_tensor(
        t["t1m"][:], t["kb"][:], t["flagc"][:], op=Op.mult), deps=[a], inc="sv")
    c = p.op("vector", lambda e: e.tensor_scalar(
        t["magic"][:], t["t1m"][:], M32, None, op0=Op.add), deps=[b], inc="sv")
    return p.op("vector", lambda e: e.tensor_scalar(
        t["nmagic"][:], t["magic"][:], -1.0, None, op0=Op.mult), deps=[c], inc="sv")


def emit_quant(p, v_ap, q_ap, nblk, s, magic_ap, magic_tok, vdep, war=()):
    toks = emit_quant_scales(p, v_ap, nblk, s, magic_ap, magic_tok, vdep, war)
    return emit_quant_apply(p, v_ap, q_ap, 0, nblk, s, s["tmp1"], magic_ap, toks)


def emit_quant_scales(p, v_ap, nblk, s, magic_ap, magic_tok, vdep, war=()):
    dd = list(vdep) + list(war) + [magic_tok]
    t0 = p.op("vector", lambda e: e.tensor_reduce(
        s["amax"][:, :nblk], v_ap.rearrange("p (b k) -> p b k", k=32),
        axis=AX, op=Op.max, apply_absolute_value=True), deps=dd, inc="sv")
    return emit_quant_scales_pre(p, nblk, s, magic_ap, [t0])


def emit_quant_scales_pre(p, nblk, s, magic_ap, amax_deps):
    nb = nblk
    dd = list(amax_deps)
    t1 = p.op("vector", lambda e: e.tensor_scalar(
        s["a2"][:, :nb], s["amax"][:, :nb], 2.0 ** -12, None, op0=Op.max), deps=dd, inc="sv")
    t2 = p.op("vector", lambda e: e.memset(s["pp"][:, :nb], 2.0 ** -12), deps=dd, inc="sv")
    cur = [t1, t2]
    bufa, bufb = "pp", "p2"
    for k in (16, 8, 4, 2, 1):
        a = p.op("vector", lambda e, k=k, bufa=bufa: e.tensor_scalar(
            s["p1"][:, :nb], s[bufa][:, :nb], float(2.0 ** k), None, op0=Op.mult),
            deps=cur, inc="sv")
        b = p.op("vector", lambda e: e.tensor_tensor(
            s["m"][:, :nb], s["a2"][:, :nb], s["p1"][:, :nb], op=Op.is_ge),
            deps=[a], inc="sv")
        c = p.op("vector", lambda e, k=k: e.tensor_scalar(
            s["f1"][:, :nb], s["m"][:, :nb], float(2.0 ** k - 1.0), 1.0,
            op0=Op.mult, op1=Op.add), deps=[b], inc="sv")
        d = p.op("vector", lambda e, bufa=bufa, bufb=bufb: e.tensor_tensor(
            s[bufb][:, :nb], s[bufa][:, :nb], s["f1"][:, :nb], op=Op.mult),
            deps=[c], inc="sv")
        cur = [d]
        bufa, bufb = bufb, bufa
    tb = p.op("vector", lambda e, bufa=bufa: e.tensor_scalar(
        s["scl"][:, :nb], s[bufa][:, :nb], 2.0 ** -3, None, op0=Op.mult),
        deps=cur, inc="sv")
    tc = p.op("vector", lambda e, bufa=bufa: e.reciprocal(s["rc"][:, :nb], s[bufa][:, :nb]),
              deps=cur, inc="sv")
    te = p.op("vector", lambda e: e.tensor_scalar(
        s["inv"][:, :nb], s["rc"][:, :nb], 2.0 ** 3, None, op0=Op.mult),
        deps=[tc], inc="sv")
    return [te, tb]


def emit_quant_apply(p, v_ap, q_ap, boff, nb, s, tmp1, magic_ap, deps):
    vB = v_ap.rearrange("p (b k) -> p k b", k=32)
    t1B = tmp1.rearrange("p (b k) -> p k b", k=32)
    invB = s["inv"][:, boff:boff + nb].unsqueeze(1).broadcast_to([128, 32, nb])
    sclB = s["scl"][:, boff:boff + nb].unsqueeze(1).broadcast_to([128, 32, nb])
    tf = p.op("vector", lambda e: e.scalar_tensor_tensor(
        t1B, vB, 1.0, invB, op0=Op.mult, op1=Op.mult), deps=deps, inc="sv")
    tg = p.op("vector", lambda e: e.tensor_scalar(
        v_ap, tmp1, magic_ap, magic_ap,
        op0=Op.add, op1=Op.subtract), deps=[tf], inc="sv")
    t2B = v_ap.rearrange("p (b k) -> p k b", k=32)
    qB = q_ap.rearrange("p (b k) -> p k b", k=32)
    th = p.op("vector", lambda e: e.scalar_tensor_tensor(
        qB, t2B, 1.0, sclB, op0=Op.mult, op1=Op.mult), deps=[tg], inc="sv")
    return th


def emit_quant_apply_x(p, v_ap, q_ap, boff, nb, s, tmp1, xbuf, magic_ap, deps,
                       nmagic_ap=None):
    invX = s["inv"][:, boff:boff + nb].unsqueeze(-1).broadcast_to([128, nb, 32])
    sclX = s["scl"][:, boff:boff + nb].unsqueeze(-1).broadcast_to([128, nb, 32])
    x3 = xbuf.rearrange("p (b k) -> p b k", k=32)
    e1 = p.op("vector", lambda e: e.tensor_copy(x3, invX), deps=deps, inc="sv")
    tf = p.op("vector", lambda e: e.scalar_tensor_tensor(
        tmp1, v_ap, 1.0, xbuf, op0=Op.mult, op1=Op.mult), deps=[e1], inc="sv")
    e2 = p.op("vector", lambda e: e.tensor_copy(x3, sclX), deps=[tf], inc="sv")
    if nmagic_ap is not None:
        tg1 = p.op("scalar", lambda e: e.activation(
            v_ap, tmp1, AF.Identity, bias=magic_ap), deps=[tf], inc="ss")
        tg = p.op("scalar", lambda e: e.activation(
            v_ap, v_ap, AF.Identity, bias=nmagic_ap), deps=[tg1], inc="ss")
    else:
        tg = p.op("vector", lambda e: e.tensor_scalar(
            v_ap, tmp1, magic_ap, magic_ap, op0=Op.add, op1=Op.subtract),
            deps=[tf], inc="sv")
    th = p.op("vector", lambda e: e.scalar_tensor_tensor(
        q_ap, v_ap, 1.0, xbuf, op0=Op.mult, op1=Op.mult), deps=[tg, e2], inc="sv")
    return th


def emit_ln_wave(p, xs, outs, st, deps_list, war=()):
    W = len(xs)
    rtoks = []
    stoks = []
    for i, x in enumerate(xs):
        rtoks.append(p.op("vector", lambda e, x=x, i=i: e.tensor_reduce(
            st["srow"][:, i:i + 1], x, axis=AX, op=Op.add, negate=True),
            deps=list(deps_list[i]) + list(war), inc="sv"))
        two = st.get("dump2") is not None
        back = 2 if two else 1
        dw = [stoks[i - back]] if i >= back else list(war)
        stoks.append(p.op("scalar", lambda e, x=x, i=i, two=two: e.activation(
            st["dump2" if (two and i % 2 == 1) else "dump"], x, AF.Square,
            accum_out=st["ssq"][:, i:i + 1]),
            deps=list(deps_list[i]) + dw, inc="ss"))
    a = p.op("vector", lambda e: e.tensor_scalar(
        st["nmean"][:, :W], st["srow"][:, :W], 1.0 / D, None, op0=Op.mult),
        deps=[rtoks[-1]], inc="sv")
    b = p.op("vector", lambda e: e.tensor_scalar(
        st["t1"][:, :W], st["ssq"][:, :W], 1.0 / D, None, op0=Op.mult),
        deps=[stoks[-1]], inc="sv")
    c = p.op("vector", lambda e: e.tensor_tensor(
        st["sq2"][:, :W], st["nmean"][:, :W], st["nmean"][:, :W], op=Op.mult),
        deps=[a], inc="sv")
    d = p.op("vector", lambda e: e.tensor_tensor(
        st["var"][:, :W], st["t1"][:, :W], st["sq2"][:, :W], op=Op.subtract),
        deps=[b, c], inc="sv")
    f = p.op("scalar", lambda e: e.activation(
        st["lnv"][:, :W], st["var"][:, :W], AF.Ln, bias=st["epsv"][:]),
        deps=[d], inc="ss")
    g = p.op("scalar", lambda e: e.activation(
        st["rstd"][:, :W], st["lnv"][:, :W], AF.Exp, scale=-0.5), deps=[f], inc="ss")
    h = p.op("vector", lambda e: e.tensor_tensor(
        st["bias2"][:, :W], st["nmean"][:, :W], st["rstd"][:, :W], op=Op.mult),
        deps=[g], inc="sv")
    for i, x in enumerate(xs):
        tok = None
        for o_ap in outs[i]:
            tok = p.op("scalar", lambda e, o=o_ap, x=x, i=i: e.activation(
                o, x, AF.Identity, scale=st["rstd"][:, i:i + 1],
                bias=st["bias2"][:, i:i + 1]), deps=[h], inc="ss")
        rtoks[i] = tok
    return rtoks


def emit_layernorm(p, x_ap, outs, t, xdep, war=()):
    dd = list(xdep) + list(war)
    a = p.op("vector", lambda e: e.tensor_reduce(
        t["srow"][:], x_ap, axis=AX, op=Op.add, negate=True), deps=dd, inc="sv")
    b = p.op("vector", lambda e: e.tensor_scalar(
        t["nmean"][:], t["srow"][:], 1.0 / D, None, op0=Op.mult), deps=[a], inc="sv")
    c = p.op("scalar", lambda e: e.activation(
        t["dump"], x_ap, AF.Square, accum_out=t["ssq"][:]), deps=dd, inc="ss")
    d = p.op("vector", lambda e: e.tensor_scalar(
        t["t1"][:], t["ssq"][:], 1.0 / D, None, op0=Op.mult), deps=[c], inc="sv")
    f = p.op("vector", lambda e: e.tensor_tensor(
        t["sq2"][:], t["nmean"][:], t["nmean"][:], op=Op.mult), deps=[b], inc="sv")
    g = p.op("vector", lambda e: e.tensor_tensor(
        t["var"][:], t["t1"][:], t["sq2"][:], op=Op.subtract), deps=[d, f], inc="sv")
    h = p.op("scalar", lambda e: e.activation(
        t["std"][:], t["var"][:], AF.Ln, bias=t["epsv"][:]), deps=[g], inc="ss")
    i = p.op("scalar", lambda e: e.activation(
        t["rstd"][:], t["std"][:], AF.Exp, scale=-0.5), deps=[h], inc="ss")
    j = p.op("vector", lambda e: e.tensor_tensor(
        t["bias2"][:], t["nmean"][:], t["rstd"][:], op=Op.mult), deps=[i], inc="sv")
    toks = []
    for o_ap in outs:
        toks.append(p.op("scalar", lambda e, o=o_ap: e.activation(
            o, x_ap, AF.Identity, scale=t["rstd"][:], bias=t["bias2"][:]),
            deps=[j], inc="ss"))
    return toks


def build_nc(seq):
    NT = seq // 128 // NCORES
    PROD = seq // NCORES
    PT = PROD // 128
    NKT = seq // 128
    NMS = NKT // 8
    WSH = HID // NCORES
    WT = WSH // 128

    nc = bass.Bass("TRN2", target_bir_lowering=False, detect_race_conditions=True)
    dt = nc.dram_tensor
    x_q = dt("x_q", [NT, 128, D], F32, kind="ExternalInput").ap()
    x_p = dt("x_p", [PROD, D], F32, kind="ExternalInput").ap()
    wq_d = dt("wq", [D, D], F32, kind="ExternalInput").ap()
    wk_d = dt("wk", [D, D], F32, kind="ExternalInput").ap()
    wv_d = dt("wv", [D, D], F32, kind="ExternalInput").ap()
    wfca_d = dt("wfca", [D, D], F32, kind="ExternalInput").ap()
    wfc_sh = dt("wfc_sh", [WSH, D], F32, kind="ExternalInput").ap()
    wproj_sh = dt("wproj_sh", [D, WSH], F32, kind="ExternalInput").ap()
    bfc_r = dt("bfc_r", [128, HID], F32, kind="ExternalInput").ap()
    bproj_r = dt("bproj_r", [128, D], F32, kind="ExternalInput").ap()
    bfca_r = dt("bfca_r", [128, D], F32, kind="ExternalInput").ap()
    gamma_r = dt("gamma_r", [128, D], F32, kind="ExternalInput").ap()
    beta_r = dt("beta_r", [128, D], F32, kind="ExternalInput").ap()
    ident_bf_d = dt("ident_bf", [128, 128], BF16, kind="ExternalInput").ap()
    masks_d = dt("masks", [8, 128, 128], BF16, kind="ExternalInput").ap()
    gamma_col = dt("gamma_col", [128, DC], F32, kind="ExternalInput").ap()
    beta_col = dt("beta_col", [128, DC], F32, kind="ExternalInput").ap()
    out_d = dt("out", [NT, 128, D], F32, kind="ExternalOutput").ap()

    cc_wi = dt("cc_wi", [2 * D * WSH], BF16)
    cc_wg = dt("cc_wg", [NCORES, 2 * D * WSH], BF16)
    cc_kvi = dt("cc_kvi", [2 * D * PROD], BF16)
    cc_kvg = dt("cc_kvg", [NCORES, 2 * D * PROD], BF16)
    wfc_in_v = cc_wi.ap()[0:D * WSH].rearrange("(a b) -> a b", b=WSH)
    wpj_in_v = cc_wi.ap()[D * WSH:2 * D * WSH].rearrange("(a b) -> a b", b=D)
    kt_in_v = cc_kvi.ap()[0:D * PROD].rearrange("(a b) -> a b", b=PROD)
    v_in_v = cc_kvi.ap()[D * PROD:2 * D * PROD].rearrange("(a b) -> a b", b=D)

    def wfc_g(g):
        return cc_wg.ap()[g][0:D * WSH].rearrange("(a b) -> a b", b=WSH)

    def wpj_g(g):
        return cc_wg.ap()[g][D * WSH:2 * D * WSH].rearrange("(a b) -> a b", b=D)

    def kt_g(r):
        return cc_kvg.ap()[r][0:D * PROD].rearrange("(a b) -> a b", b=PROD)

    def v_g(r):
        return cc_kvg.ap()[r][D * PROD:2 * D * PROD].rearrange("(a b) -> a b", b=D)
    sp_qt = dt("sp_qt", [D, PROD], BF16)
    sp_xnqt = dt("sp_xnqt", [NT, 128, DC * 128], BF16)
    sp_vq = dt("sp_vq", [NT, 128, D], BF16)
    sp_g = dt("sp_g", [NT, 128, D], BF16)

    from contextlib import ExitStack
    with ExitStack() as ST:
        def sbuf(name, shape, dtp=F32):
            return ST.enter_context(nc.sbuf_tensor(name, shape, dtp))

        tiny = {k: sbuf("m_" + k, [128, 1]) for k in ["ones", "kb", "flagc", "t1m", "magic", "nmagic"]}
        ident_bf = sbuf("identb", [128, 128], BF16)
        masks = sbuf("masks_s", [128, 8 * 128], BF16)
        onecol = sbuf("onecol", [128, 1], BF16)
        onesrow = sbuf("onesrow", [32, 128], BF16)
        gcol_sb = sbuf("gcol_sb", [128, DC])
        bcol_sb = sbuf("bcol_sb", [128, DC])
        bcol_bf = sbuf("bcol_bf", [128, DC], BF16)
        rl = sbuf("rl", [128, 1])
        lnt = {k: sbuf("ln_" + k, [128, 1]) for k in
               ["srow", "nmean", "t1", "sq2", "var", "std", "rstd", "bias2", "ssq", "epsv"]}
        lndump = sbuf("ln_dump", [128, D])
        qsc0 = {k: sbuf("qs_" + k, [128, 128]) for k in
                ["amax", "a2", "cf1", "cf2", "c", "p0", "p1", "p2", "pp", "m",
                 "f1", "f2", "rc", "scl", "inv"]}

        SLA_N = 25088
        SLB_N = 23552
        slabA = sbuf("slabA", [128, SLA_N])
        slabB = sbuf("slabB", [128, SLB_N])
        psum = ST.enter_context(nc.psum_tensor("ps", [128, 4096], F32))

        sems = {}
        for s in ALLSEMS:
            sems[s] = ST.enter_context(nc.semaphore(s))

        p = Prog()
        LA = Layout(slabA.ap(), SLA_N)
        LB = Layout(slabB.ap(), SLB_N)
        PS = psum.ap()
        PSbf = PS.bitcast(BF16)

        mg_tok = emit_magic(p, tiny)
        magic_ap = tiny["magic"][:]
        nmagic_ap = tiny["nmagic"][:]
        t_id = p.dma("sync", lambda e: e.dma_start(out=ident_bf[:], in_=ident_bf_d), "dls")
        t_mask = p.dma("sync", lambda e: e.dma_start(
            out=masks.ap().rearrange("p (r q) -> p r q", r=8), in_=masks_d.rearrange("r p q -> p r q")), "dls")
        oc = p.op("vector", lambda e: e.memset(onecol[:], 1.0), inc="sv")
        p.op("vector", lambda e: e.memset(lnt["epsv"][:], 1e-5), inc="sv")
        onecol_tok = oc

        LA.reset(); LB.reset()
        w_bf = {w: LA.bf16(DC * D) for w in ["wq", "wk", "wv"]}
        wdeps = {}
        for wname, dram in [("wq", wq_d), ("wk", wk_d), ("wv", wv_d)]:
            wdeps[wname] = p.dma("gpsimd", lambda e, w=wname, dr=dram: e.dma_start(
                out=w_bf[w].rearrange("p (dc k) -> p dc k", dc=DC),
                in_=dr.rearrange("(dc p) k -> p dc k", p=128)), "dlg")
        a_w32 = LA.f32(WT * D)
        a_wq = LA.bf16(WT * D)
        a_p32 = LA.f32(DC * WSH)
        a_pq = LA.bf16(DC * WSH)
        a_tmp = LB.f32(D)
        a_xbuf = LB.f32(D)
        a_stage = [LB.bf16(DC * 128) for _ in range(2)]

        t_w = p.dma("sync", lambda e: e.dma_start(
            out=a_w32.rearrange("p (t d) -> p t d", t=WT),
            in_=wfc_sh.rearrange("(t p) d -> p t d", p=128)), "dls")
        t_p = p.dma("sync", lambda e: e.dma_start(
            out=a_p32.rearrange("p (t w) -> p t w", t=DC),
            in_=wproj_sh.rearrange("(t p) w -> p t w", p=128)), "dls")

        NBW = D // 32
        NBP = WSH // 32
        am = []
        for i in range(WT):
            am.append(p.op("vector", lambda e, i=i: e.tensor_reduce(
                qsc0["amax"][:, i * NBW:(i + 1) * NBW],
                a_w32[:, i * D:(i + 1) * D].rearrange("p (b k) -> p b k", k=32),
                axis=AX, op=Op.max, apply_absolute_value=True),
                deps=[t_w, mg_tok], inc="sv"))
        tsc = emit_quant_scales_pre(p, WT * NBW, qsc0, magic_ap, am)
        tq_toks = []
        last_q = None
        for i in range(WT):
            tok = emit_quant_apply_x(p, a_w32[:, i * D:(i + 1) * D],
                                     a_wq[:, i * D:(i + 1) * D], i * NBW, NBW,
                                     qsc0, a_tmp, a_xbuf, magic_ap,
                                     tsc + ([last_q] if last_q else []))
            tq_toks.append(tok); last_q = tok
        am = []
        for i in range(DC):
            am.append(p.op("vector", lambda e, i=i: e.tensor_reduce(
                qsc0["amax"][:, i * NBP:(i + 1) * NBP],
                a_p32[:, i * WSH:(i + 1) * WSH].rearrange("p (b k) -> p b k", k=32),
                axis=AX, op=Op.max, apply_absolute_value=True),
                deps=[t_p, last_q], inc="sv"))
        tsc = emit_quant_scales_pre(p, DC * NBP, qsc0, magic_ap, am)
        tp_toks = []
        for i in range(DC):
            tok = emit_quant_apply_x(p, a_p32[:, i * WSH:(i + 1) * WSH],
                                     a_pq[:, i * WSH:(i + 1) * WSH], i * NBP, NBP,
                                     qsc0, a_tmp[:, 0:WSH], a_xbuf[:, 0:WSH],
                                     magic_ap, tsc + [last_q])
            tp_toks.append(tok); last_q = tok

        ev_war = None
        dma_war = {}
        for i in range(WT):
            sg = a_stage[i % 2]
            tts = []
            for dci in range(DC):
                deps = [tq_toks[i], t_id] + ([ev_war] if ev_war else [])
                tts.append(p.op("tensor", lambda e, i=i, dci=dci: e.transpose(
                    PSbf[:, (i % 2) * 1024 + dci * 128: (i % 2) * 1024 + (dci + 1) * 128],
                    a_wq[:, i * D + dci * 128: i * D + (dci + 1) * 128],
                    ident_bf[:]), deps=deps, inc="pe"))
            ev = p.op("scalar", lambda e, i=i, sg=sg: e.activation(
                sg[:, 0:DC * 128], PSbf[:, (i % 2) * 1024:(i % 2) * 1024 + DC * 128],
                AF.Identity),
                deps=[tts[-1]] + ([dma_war.get(i - 2)] if (i - 2) in dma_war else []),
                inc="ss")
            ev_war = ev
            dma_war[i] = p.dma("gpsimd", lambda e, i=i, sg=sg: e.dma_start(
                out=wfc_in_v.rearrange("(dc q) h -> q dc h", q=128)[:, :, i * 128:(i + 1) * 128],
                in_=sg.rearrange("p (dc t) -> p dc t", dc=DC)), "dlg", deps=[ev])
        for i in range(DC):
            j = WT + i
            sg = a_stage[j % 2]
            tts = []
            for hci in range(WT):
                deps = [tp_toks[i], t_id] + ([ev_war] if ev_war else [])
                tts.append(p.op("tensor", lambda e, i=i, hci=hci, j=j: e.transpose(
                    PSbf[:, (j % 2) * 1024 + hci * 128: (j % 2) * 1024 + (hci + 1) * 128],
                    a_pq[:, i * WSH + hci * 128: i * WSH + (hci + 1) * 128],
                    ident_bf[:]), deps=deps, inc="pe"))
            ev = p.op("scalar", lambda e, i=i, j=j, sg=sg: e.activation(
                sg[:, 0:WSH], PSbf[:, (j % 2) * 1024:(j % 2) * 1024 + WSH], AF.Identity),
                deps=[tts[-1]] + ([dma_war.get(j - 2)] if (j - 2) in dma_war else []),
                inc="ss")
            ev_war = ev
            dma_war[j] = p.dma("gpsimd", lambda e, i=i, sg=sg: e.dma_start(
                out=wpj_in_v.rearrange("(hc q) dd -> q hc dd", q=128)[:, :, i * 128:(i + 1) * 128],
                in_=sg[:, 0:WSH].rearrange("p (hc t) -> p hc t", hc=WT)), "dlg", deps=[ev])

        dlg_now = ("dlg", p.count["dlg"])

        bar = p.barrier()
        LA.reset(); LB.reset()
        LA.f32(3 * (DC * D // 2))
        WV = min(4, PT)
        xa = [LA.f32(D) for _ in range(WV)]
        xnf = [LA.f32(D) for _ in range(WV)]
        xnbf = [LA.bf16(D) for _ in range(WV)]
        xnq = [LB.bf16(D) for _ in range(WV)]
        xnT = LB.bf16(DC * PROD)
        stg = [LB.bf16(512) for _ in range(4)]
        stg2 = [LB.bf16(D) for _ in range(4)]
        q_tmp = LB.f32(D)
        q_xbuf = LB.f32(D)
        lndump2 = LB.f32(D)
        xnqt_stage = [LB.bf16(DC * 128) for _ in range(4)]
        xa2 = [LB.f32(D) for _ in range(WV)]
        lnw = [dict({k: sbuf(f"lw{i}_{k}", [128, WV]) for k in
                     ["srow", "nmean", "t1", "sq2", "var", "lnv", "rstd", "bias2", "ssq"]},
                    dump=lndump.ap(), dump2=lndump2, epsv=lnt["epsv"]) for i in range(2)]

        xp_t = x_p.rearrange("(t p) d -> t p d", p=128)
        NW = PT // WV
        WTOK = 128 * WV

        qtoks = {}
        pstate = {"stg_dma": {}, "stg2_dma": {}, "xq_dma": {}, "kcev": None,
                  "vev": None, "xqev": None}

        def emit_projchunk(wname, dest_cc, toff, ntok, dep_xnT):
            st_ = pstate
            for kc in range(DC):
                sg = stg[kc % 4]
                sem = f"ds{kc % 4}"
                bank = (kc % 2) * 512
                mms = []
                for dci in range(DC):
                    deps = [dep_xnT, wdeps[wname]] +                         ([st_["kcev"]] if st_["kcev"] else [])
                    mms.append(p.op("tensor", lambda e, kc=kc, dci=dci, bank=bank,
                                    toff=toff, ntok=ntok: e.matmul(
                        PS[:, bank:bank + ntok],
                        w_bf[wname][:, dci * D + kc * 128: dci * D + (kc + 1) * 128],
                        xnT[:, dci * PROD + toff: dci * PROD + toff + ntok],
                        start=(dci == 0), stop=(dci == DC - 1)),
                        deps=deps, inc="pe" if dci == DC - 1 else None))
                ev = p.op("scalar", lambda e, sg=sg, bank=bank, ntok=ntok: e.activation(
                    sg[:, 0:ntok], PS[:, bank:bank + ntok], AF.Identity),
                    deps=[mms[-1], st_["stg_dma"].get(kc % 4)], inc="ss")
                st_["kcev"] = ev
                st_["stg_dma"][kc % 4] = p.dma("gpsimd", lambda e, kc=kc, sg=sg,
                                               toff=toff, ntok=ntok: e.dma_start(
                    out=dest_cc[kc * 128:(kc + 1) * 128, toff:toff + ntok],
                    in_=sg[:, 0:ntok]), sem, deps=[ev])

        def emit_vtile(tt, dest_fn, dep_xnT):
            st_ = pstate
            sg = stg2[tt % 4]
            sem = f"dv{tt % 4}"
            base = 2048 + (tt % 2) * 1024
            mms = []
            for dci in range(DC):
                for h in range(2):
                    deps = [dep_xnT, wdeps["wv"]] + ([st_["vev"]] if st_["vev"] else [])
                    mms.append(p.op("tensor", lambda e, tt=tt, dci=dci, h=h, base=base: e.matmul(
                        PS[:, base + h * 512: base + (h + 1) * 512],
                        xnT[:, dci * PROD + tt * 128: dci * PROD + (tt + 1) * 128],
                        w_bf["wv"][:, dci * D + h * 512: dci * D + (h + 1) * 512],
                        start=(dci == 0), stop=(dci == DC - 1)),
                        deps=deps, inc="pe" if (dci == DC - 1 and h == 1) else None))
            ev = p.op("scalar", lambda e, sg=sg, base=base: e.activation(
                sg[:, 0:D], PS[:, base:base + D], AF.Identity),
                deps=[mms[-1], st_["stg2_dma"].get(tt % 4)], inc="ss")
            st_["vev"] = ev
            st_["stg2_dma"][tt % 4] = dest_fn(tt, sg, ev, sem)

        def emit_rows(which, entry_war=()):
            ln_prev = {}
            ev_war = None
            q_last_l = None
            for w in range(NW):
                tiles = [w * WV + i for i in range(WV)]
                st = lnw[w % 2]
                xw = xa if w % 2 == 0 else xa2
                ldeps = []
                for i, tt in enumerate(tiles):
                    if (w - 2) in ln_prev:
                        war = [ln_prev.get(w - 2)]
                    elif which == "prod":
                        war = [tq_toks[-1]] if w % 2 == 0 else []
                    else:
                        war = [prod_ln]
                    xsem = f"dx{w % 2}"
                    if which == "prod":
                        tok = p.dma("sync", lambda e, tt=tt, i=i, xw=xw: e.dma_start(
                            out=xw[i], in_=xp_t[tt]), xsem, deps=war)
                    else:
                        tok = p.dma("sync", lambda e, tt=tt, i=i, xw=xw: e.dma_start(
                            out=xw[i], in_=x_q[tt]), xsem, deps=war)
                    ldeps.append([tok])
                outs = [[xnbf[i]] if which == "prod" else [xnbf[i], xnf[i]]
                        for i in range(WV)]
                lt = emit_ln_wave(p, [xw[i] for i in range(WV)], outs, st, ldeps,
                                  war=([ln_prev.get(w - 1)] if (w - 1) in ln_prev
                                       else bar + list(entry_war)))
                ln_prev[w] = lt[-1]
                if which == "query":
                    am = []
                    NBX = D // 32
                    for i in range(WV):
                        am.append(p.op("vector", lambda e, i=i: e.tensor_reduce(
                            qsc0["amax"][:, i * NBX:(i + 1) * NBX],
                            xnf[i].rearrange("p (b k) -> p b k", k=32),
                            axis=AX, op=Op.max, apply_absolute_value=True),
                            deps=[lt[i], mg_tok] + ([q_last_l] if q_last_l else []),
                            inc="sv"))
                    tsc = emit_quant_scales_pre(p, WV * NBX, qsc0, magic_ap, am)
                    for i in range(WV):
                        q_last_l = emit_quant_apply_x(
                            p, xnf[i], xnq[i], i * NBX, NBX, qsc0, q_tmp,
                            q_xbuf, magic_ap,
                            tsc + ([q_last_l] if q_last_l else []))
                        qtoks[tiles[i]] = q_last_l
                for i, tt in enumerate(tiles):
                    tts = []
                    for dci in range(DC):
                        deps = [lt[i], t_id] + ([ev_war] if ev_war else []) +                             list(entry_war)
                        tts.append(p.op("tensor", lambda e, i=i, dci=dci: e.transpose(
                            PSbf[:, 2048 + dci * 128: 2048 + (dci + 1) * 128],
                            xnbf[i][:, dci * 128:(dci + 1) * 128], ident_bf[:]),
                            deps=deps, inc="pe"))
                    ev_war = p.op("scalar", lambda e, tt=tt: e.activation(
                        xnT.rearrange("p (dc t2) -> p dc t2", dc=DC)[:, :, tt * 128:(tt + 1) * 128],
                        PSbf[:, 2048:2048 + DC * 128].rearrange("p (dc t2) -> p dc t2", dc=DC),
                        AF.Identity), deps=[tts[-1]] + list(entry_war), inc="ss")
                    if which == "query":
                        sgx = xnqt_stage[tt % 4]
                        sem = f"dq{tt % 4}"
                        tts2 = []
                        for dci in range(DC):
                            deps = [qtoks[tt], t_id] +                                 ([pstate["xqev"]] if pstate["xqev"] else [])
                            tts2.append(p.op("tensor", lambda e, i=i, dci=dci: e.transpose(
                                PSbf[:, 3072 + dci * 128: 3072 + (dci + 1) * 128],
                                xnq[i][:, dci * 128:(dci + 1) * 128], ident_bf[:]),
                                deps=deps, inc="pe"))
                        ev2 = p.op("scalar", lambda e, sgx=sgx: e.activation(
                            sgx[:, 0:DC * 128], PSbf[:, 3072:3072 + DC * 128],
                            AF.Identity),
                            deps=[tts2[-1], pstate["xq_dma"].get(tt % 4)],
                            inc="ss")
                        pstate["xqev"] = ev2
                        pstate["xq_dma"][tt % 4] = p.dma(
                            "gpsimd", lambda e, tt=tt, sgx=sgx: e.dma_start(
                                out=sp_xnqt.ap()[tt], in_=sgx[:, 0:DC * 128]), sem,
                            deps=[ev2])
                xnT_wave = ev_war
                if which == "prod":
                    for i, tt in enumerate(tiles):
                        emit_vtile(tt, lambda tt, sg, ev, sem: p.dma(
                            "gpsimd", lambda e, tt=tt, sg=sg: e.dma_start(
                                out=v_in_v[tt * 128:(tt + 1) * 128, :],
                                in_=sg[:, 0:D]), sem, deps=[ev]), xnT_wave)
                    emit_projchunk("wk", kt_in_v, w * WTOK, WTOK, xnT_wave)
                else:
                    for i, tt in enumerate(tiles):
                        emit_vtile(tt, lambda tt, sg, ev, sem: p.dma(
                            "gpsimd", lambda e, tt=tt, sg=sg: e.dma_start(
                                out=sp_vq.ap()[tt], in_=sg[:, 0:D]), sem,
                            deps=[ev]), xnT_wave)
                    emit_projchunk("wq", sp_qt.ap(), w * WTOK, WTOK, xnT_wave)
            return ev_war, ln_prev[NW - 1]

        xnT_done, prod_ln = emit_rows("prod")
        ktv_sp = [(f"ds{i}", p.count[f"ds{i}"]) for i in range(4)] + \
                 [(f"dv{i}", p.count[f"dv{i}"]) for i in range(4)]
        t_ccw = p.op("gpsimd", lambda e: e.collective_compute(
            "AllGather", Op.bypass, replica_groups=[list(range(NCORES))],
            ins=[cc_wi.ap()], outs=[cc_wg.ap()]), deps=[dlg_now], inc="cc")
        t_ccp = t_ccw

        kv_pe = ("pe", p.count["pe"])
        _, _ = emit_rows("query", entry_war=[kv_pe, prod_ln])
        q_last = qtoks[NT - 1]
        xnqt_dma = [pstate["xq_dma"][i] for i in pstate["xq_dma"]]

        bar = p.barrier()
        LA.reset(); LB.reset()
        wfcqT_sb = LA.bf16(DC * HID)
        hb2 = [LA.f32(HID // 2) for _ in range(2)]
        ffn_tmp1 = LA.f32(HID // 2)
        bfc_row = LA.bf16(HID)[0:32, :]
        bproj_row = LA.bf16(D)[0:32, :]
        wprojqT_sb = LB.bf16(HC * D)
        hq = [LB.bf16(HID) for _ in range(2)]
        ffn_xbuf = LB.f32(HID // 2)
        hqT = hq
        xnqt_t = [LB.bf16(DC * 128) for _ in range(2)]

        wf4 = wfcqT_sb.rearrange("p (dc g h) -> p dc g h", dc=DC, g=NCORES)
        wp4 = wprojqT_sb.rearrange("p (g x dd) -> p g x dd", g=NCORES, x=WT)
        def load_xnqt(u, deps):
            pp = u % 2
            return p.dma("gpsimd", lambda e, u=u, pp=pp: e.dma_start(
                out=xnqt_t[pp][:, :], in_=sp_xnqt.ap()[u]), "dlg", deps=deps)
        xnqt_ld = [None, None]
        xnqt_ld[0] = load_xnqt(0, bar + [q_last])
        xnqt_ld[1] = load_xnqt(1, bar + [q_last])
        wf_toks = []
        for g in range(NCORES):
            wf_toks.append(p.dma("gpsimd", lambda e, g=g: e.dma_start(
                out=wf4[:, :, g, :],
                in_=wfc_g(g).rearrange("(dc p) h -> p dc h", p=128)), "dlg",
                deps=[t_ccw] + bar))
        for g in range(NCORES):
            t_wp = p.dma("sync", lambda e, g=g: e.dma_start(
                out=wp4[:, g, :, :],
                in_=wpj_g(g).rearrange("(x p) dd -> p x dd", p=128)), "dls",
                deps=[t_ccp] + bar)
        z1 = p.op("vector", lambda e: e.memset(bfc_row[:, :], 0.0), deps=bar, inc="sv")
        z2 = p.op("vector", lambda e: e.memset(bproj_row[:, :], 0.0), deps=bar, inc="sv")
        t_bf = p.dma("gpsimd", lambda e: e.dma_start(out=bfc_row[0:1, :], in_=bfc_r[0:1, :]),
                     "dlg", deps=bar + [z1, z2])
        t_bp = p.dma("gpsimd", lambda e: e.dma_start(out=bproj_row[0:1, :], in_=bproj_r[0:1, :]),
                     "dlg")
        z3 = p.op("vector", lambda e: e.memset(onesrow[:, :], 0.0), deps=bar, inc="sv")
        orow = p.op("vector", lambda e: e.memset(onesrow[0:1, :], 1.0), deps=[z3], inc="sv")
        t_cckv = p.op("gpsimd", lambda e: e.collective_compute(
            "AllGather", Op.bypass, replica_groups=[list(range(NCORES))],
            ins=[cc_kvi.ap()], outs=[cc_kvg.ap()]), deps=ktv_sp, inc="cc")
        t_cckt = t_ccv = t_cckv

        evac_toks = {}
        apply_half = {}
        apply_tok = {}
        hqT_tok = {}
        g_toks = []
        projread_tok = {}
        fc_pe_tok = {}

        FCBANK = (0, 512, 1536, 2048)

        def emit_fc(u):
            pp = u % 2
            deps0 = bar + [q_last, t_bf, orow, xnqt_ld[pp]]
            for h in range(2):
                k = 2 * u + h
                hbuf = hb2[k % 2]
                evs = []
                for g in range(4):
                    gg = 4 * h + g
                    bank = FCBANK[g]
                    war = [evac_toks[k - 1][g]] if (k - 1) in evac_toks else []
                    war = war + [wf_toks[gg]]
                    mms = []
                    for dci in range(DC):
                        mms.append(p.op("tensor", lambda e, gg=gg, dci=dci, bank=bank, pp=pp: e.matmul(
                            PS[:, bank:bank + 512],
                            xnqt_t[pp][:, dci * 128:(dci + 1) * 128],
                            wfcqT_sb[:, dci * HID + gg * 512: dci * HID + (gg + 1) * 512],
                            start=(dci == 0), stop=False),
                            deps=deps0 + war, inc=None))
                    mms.append(p.op("tensor", lambda e, gg=gg, bank=bank: e.matmul(
                        PS[:, bank:bank + 512], onesrow[:],
                        bfc_row[:, gg * 512:(gg + 1) * 512],
                        start=False, stop=True), deps=deps0 + war, inc="pe"))
                    hbwar = [apply_half[k - 2]] if (k - 2) in apply_half else []
                    evs.append(p.op("scalar", lambda e, g=g, bank=bank, hbuf=hbuf: e.activation(
                        hbuf[:, g * 512:(g + 1) * 512], PS[:, bank:bank + 512],
                        AF.Identity), deps=[mms[-1]] + hbwar, inc="ss"))
                fc_pe_tok[u] = mms[-1]
                evac_toks[k] = evs

        def emit_mid(u):
            NBH = (HID // 2) // 32
            gels = []
            ams = []
            for h in range(2):
                k = 2 * u + h
                hbuf = hb2[k % 2]
                gels.append(p.op("scalar", lambda e, hbuf=hbuf: e.activation(
                    hbuf, hbuf, AF.Gelu), deps=evac_toks[k], inc="ss"))
                war = [apply_half[2 * u - 1]] if (2 * u - 1) in apply_half else []
                ams.append(p.op("vector", lambda e, hbuf=hbuf, h=h: e.tensor_reduce(
                    qsc0["amax"][:, h * NBH:(h + 1) * NBH],
                    hbuf.rearrange("p (b kk) -> p b kk", kk=32),
                    axis=AX, op=Op.max, apply_absolute_value=True),
                    deps=[gels[-1], mg_tok] + war, inc="sv"))
            tsc = emit_quant_scales_pre(p, 2 * NBH, qsc0, magic_ap, ams)
            war = []
            if (u - 2) in projread_tok:
                war.append(projread_tok[u - 2])
            prev = []
            for h in range(2):
                k = 2 * u + h
                hbuf = hb2[k % 2]
                apply_half[k] = emit_quant_apply_x(
                    p, hbuf, hq[u % 2][:, h * (HID // 2):(h + 1) * (HID // 2)],
                    h * NBH, NBH, qsc0, ffn_tmp1, ffn_xbuf, magic_ap,
                    tsc + war + prev)
                prev = [apply_half[k]]
            apply_tok[u] = apply_half[2 * u + 1]

        def emit_tail(u):
            hqu = hq[u % 2]
            ev_list = []
            for grp in range(HC // 8):
                tts = []
                for j in range(8):
                    hci = grp * 8 + j
                    deps = [apply_tok[u], t_id]
                    if ev_list:
                        deps.append(ev_list[-1])
                    elif (u - 2) in hqT_tok:
                        deps.append(hqT_tok[u - 2])
                    tts.append(p.op("tensor", lambda e, hci=hci, j=j, hqu=hqu: e.transpose(
                        PSbf[:, 2048 + j * 128: 2048 + (j + 1) * 128],
                        hqu[:, hci * 128:(hci + 1) * 128], ident_bf[:]),
                        deps=deps, inc="pe"))
                ev_list.append(p.op("scalar", lambda e, grp=grp, hqu=hqu: e.activation(
                    hqu[:, grp * 1024:(grp + 1) * 1024],
                    PSbf[:, 2048:2048 + 1024], AF.Identity), deps=[tts[-1]], inc="ss"))
            hqT_tok[u] = ev_list[-1]
            proj_mms = []
            war = [projread_tok[u - 1]] if (u - 1) in projread_tok else []
            for hci in range(HC):
                for h in range(2):
                    deps = [ev_list[-1], t_wp, t_bp] + war
                    proj_mms.append(p.op("tensor", lambda e, hci=hci, h=h, hqu=hqu: e.matmul(
                        PS[:, 3072 + h * 512: 3072 + (h + 1) * 512],
                        hqu[:, hci * 128:(hci + 1) * 128],
                        wprojqT_sb[:, hci * D + h * 512: hci * D + (h + 1) * 512],
                        start=(hci == 0), stop=False), deps=deps, inc=None))
            for h in range(2):
                proj_mms.append(p.op("tensor", lambda e, h=h: e.matmul(
                    PS[:, 3072 + h * 512: 3072 + (h + 1) * 512], onesrow[:],
                    bproj_row[:, h * 512:(h + 1) * 512],
                    start=False, stop=True), deps=[t_bp] + war,
                    inc="pe" if h == 1 else None))
            t_mm = proj_mms[-1]
            gout = xnqt_t[u % 2]
            t_g = p.op("vector", lambda e, gout=gout: e.tensor_copy(
                gout, PS[:, 3072:3072 + D]),
                deps=[t_mm, fc_pe_tok[u]], inc="sv")
            projread_tok[u] = t_g
            g_toks.append(p.dma("gpsimd", lambda e, u=u, gout=gout: e.dma_start(
                out=sp_g.ap()[u], in_=gout), "dlg", deps=[t_g]))

        for u in range(NT):
            pp = u % 2
            if u >= 2:
                war_x = [fc_pe_tok[u - 2]] if (u - 2) in fc_pe_tok else []
                if u - 4 >= 0:
                    war_x.append(g_toks[u - 4])
                xnqt_ld[pp] = load_xnqt(u, bar + [q_last] + war_x)
            emit_fc(u)
            if u >= 2:
                emit_tail(u - 2)
            emit_mid(u)
        emit_tail(NT - 2)
        emit_tail(NT - 1)
        t_g = g_toks[-1]

        bar = p.barrier()
        LA.reset(); LB.reset()
        CKT = min(4, PROD // 128)
        CW = 128 * CKT
        CPM0 = 8 // CKT
        NBK = 2 * CPM0
        NBV = 2 * CPM0 - 1
        NCH = NKT // CKT
        acc = LA.f32(NT * 1025)
        ktb = [LA.bf16(DC * CW) for _ in range(NBK)]
        vtb = [LB.bf16(CKT * D) for _ in range(NBV)]
        QTs = LB.bf16(DC * PROD)
        expT = LB.bf16(8 * PROD)
        tmpb = LB.bf16(128)

        t_qt = p.dma("gpsimd", lambda e: e.dma_start(
            out=QTs.rearrange("p (kc t) -> p kc t", kc=DC),
            in_=sp_qt.ap().rearrange("(kc p) t -> p kc t", p=128)), "dlg", deps=bar)
        accz = p.op("vector", lambda e: e.memset(acc, 0.0), deps=bar, inc="sv")

        def load_k(q, deps):
            r = (CW * q) // PROD
            l0 = (CW * q) % PROD
            return p.dma("sync", lambda e, r=r, l0=l0, q=q: e.dma_start(
                out=ktb[q % NBK].rearrange("p (dc w) -> p dc w", dc=DC),
                in_=kt_g(r)[:, l0:l0 + CW].rearrange("(dc p) w -> p dc w", p=128)),
                "dls", deps=deps + [t_cckt])

        def load_v(q, deps):
            r = (CW * q) // PROD
            l0 = (CW * q) % PROD
            return p.dma("gpsimd", lambda e, r=r, l0=l0, q=q: e.dma_start(
                out=vtb[q % NBV].rearrange("p (s dd) -> p s dd", s=CKT),
                in_=v_g(r)[l0:l0 + CW, :].rearrange("(s p) dd -> p s dd", p=128)),
                f"dw{(q // CPM0) % 2}", deps=deps + [t_ccv])

        CPM = CPM0
        ktoks = {}
        vtoks = {}
        for q in range(min(2 * CPM, NCH)):
            ktoks[q] = load_k(q, bar)
        for q in range(min(NBV, NCH)):
            vtoks[q] = load_v(q, bar)
        chunk_pe = {}
        attv_pe = {}
        exp_toks = {}
        tm_prev = [None]
        merge_tok = [accz] * NT
        slot_war = [None, None]
        sc_war = [None, None]
        for ms in range(NMS):
            for kt in range(8 * ms, 8 * ms + 8):
                q = kt // CKT
                tk = ktoks[q]
                ub = kt // 8
                qoff = 128 * ub
                chunks = []
                pos = qoff
                while pos < PROD:
                    ln = min(512, PROD - pos)
                    chunks.append((pos, ln))
                    pos += ln
                for ci, (qs, ln) in enumerate(chunks):
                    bank = (ci % 2) * 512
                    mms = []
                    for dci in range(DC):
                        deps = [tk, t_qt] + ([sc_war[ci % 2]] if sc_war[ci % 2] else [])
                        mms.append(p.op(
                            "tensor",
                            lambda e, kt=kt, dci=dci, qs=qs, ln=ln, bank=bank: e.matmul(
                                PS[:, bank:bank + ln],
                                ktb[(kt // CKT) % NBK][:, dci * CW + (kt % CKT) * 128:
                                                       dci * CW + (kt % CKT) * 128 + 128],
                                QTs[:, dci * PROD + qs: dci * PROD + qs + ln],
                                start=(dci == 0), stop=(dci == DC - 1)),
                            deps=deps, inc="pe" if dci == DC - 1 else None))
                    chunk_pe[q] = mms[-1]
                    col0 = (kt % 8) * PROD + qs
                    if ci == 0:
                        t_e0 = p.op("scalar", lambda e, bank=bank: e.activation(
                            tmpb[:, 0:128], PS[:, bank:bank + 128], AF.Exp,
                            scale=1.0 / 32.0), deps=[mms[-1], tm_prev[0]], inc="ss")
                        t_m = p.op("vector", lambda e, kt=kt, col0=col0: e.tensor_tensor(
                            expT[:, col0:col0 + 128], tmpb[:, 0:128],
                            masks[:, (kt % 8) * 128:(kt % 8) * 128 + 128], op=Op.mult),
                            deps=[t_e0, t_mask], inc="sv")
                        tm_prev[0] = t_m
                        ek = [t_m]
                        if ln > 128:
                            t_e1 = p.op("scalar",
                                        lambda e, bank=bank, ln=ln, col0=col0: e.activation(
                                            expT[:, col0 + 128: col0 + ln],
                                            PS[:, bank + 128:bank + ln], AF.Exp,
                                            scale=1.0 / 32.0), deps=[mms[-1]], inc="ss")
                            ek.append(t_e1)
                            sc_war[ci % 2] = t_e1
                        else:
                            sc_war[ci % 2] = t_e0
                        exp_toks[kt] = ek
                    else:
                        t_e = p.op("scalar",
                                   lambda e, bank=bank, ln=ln, col0=col0: e.activation(
                                       expT[:, col0: col0 + ln], PS[:, bank:bank + ln],
                                       AF.Exp, scale=1.0 / 32.0), deps=[mms[-1]], inc="ss")
                        sc_war[ci % 2] = t_e
                        exp_toks[kt].append(t_e)
            for q in range(CPM * (ms + 2), CPM * (ms + 3)):
                if q < NCH and q not in ktoks:
                    war = [chunk_pe.get(q - NBK)]
                    ktoks[q] = load_k(q, [w for w in war if w])
            for u in range(ms, NT):
                sl = u % 2
                base = 1024 + sl * 1536
                first = True
                t_mm = None
                for kt in range(8 * ms, 8 * ms + 8):
                    deps = list(exp_toks[kt]) + [vtoks[kt // CKT]]
                    if slot_war[sl] is not None:
                        deps.append(slot_war[sl])
                    ecol = (kt % 8) * PROD + u * 128
                    vt = vtb[(kt // CKT) % NBV]
                    for part in range(2):
                        p.op("tensor",
                             lambda e, kt=kt, part=part, base=base, ecol=ecol, vt=vt, first=first, ms=ms: e.matmul(
                                 PS[:, base + part * 512: base + (part + 1) * 512],
                                 expT[:, ecol:ecol + 128],
                                 vt[:, (kt % CKT) * D + part * 512:(kt % CKT) * D + (part + 1) * 512],
                                 start=first, stop=(kt == 8 * ms + 7)),
                             deps=deps, inc=None)
                    t_mm = p.op("tensor",
                                lambda e, kt=kt, base=base, ecol=ecol, first=first, ms=ms: e.matmul(
                                    PS[:, base + 1024: base + 1025],
                                    expT[:, ecol:ecol + 128], onecol[:],
                                    start=first, stop=(kt == 8 * ms + 7)),
                                deps=deps + [onecol_tok],
                                inc="pe" if kt == 8 * ms + 7 else None)
                    first = False
                mtk = p.op("vector", lambda e, u=u, base=base: e.tensor_tensor(
                    acc[:, u * 1025:(u + 1) * 1025], acc[:, u * 1025:(u + 1) * 1025],
                    PS[:, base:base + 1025], op=Op.add),
                    deps=[t_mm, merge_tok[u]], inc="sv")
                merge_tok[u] = mtk
                slot_war[sl] = mtk
                for q in range(CPM * ms, CPM * (ms + 1)):
                    attv_pe[q] = t_mm
            for q in range(CPM * (ms + 1), CPM * (ms + 3)):
                if q < NCH and q not in vtoks:
                    if q - NBV >= 0 and attv_pe.get(q - NBV) is None:
                        continue
                    war = [attv_pe.get(q - NBV)]
                    vtoks[q] = load_v(q, [w for w in war if w])


        bar = p.barrier()
        LB.reset()
        W4 = min(4, NT)
        wfca_sb = LB.bf16(DC * D)
        bfca2_row = LB.bf16(D)[0:32, :]
        t4A = [LB.f32(D) for _ in range(W4)]
        bfrow_f32 = t4A[0][0:1, :]
        t4B = [LB.f32(D) for _ in range(W4)]
        xt4 = [LB.f32(D) for _ in range(W4)]
        gt = [LB.bf16(D) for _ in range(W4)]
        vt4 = [LB.bf16(D) for _ in range(W4)]
        zt = vt4
        zT = [LB.bf16(DC * 128) for _ in range(W4)]
        rl4 = sbuf("rl4", [128, W4])
        lnw4 = [dict(d, dump2=None) for d in lnw]

        t_wfca = p.dma("gpsimd", lambda e: e.dma_start(
            out=wfca_sb.rearrange("p (dc k) -> p dc k", dc=DC),
            in_=wfca_d.rearrange("(dc p) k -> p dc k", p=128)), "dlg", deps=bar)
        t_gc = p.dma("sync", lambda e: e.dma_start(out=gcol_sb[:], in_=gamma_col), "dls",
                     deps=bar)
        t_bc0 = p.dma("sync", lambda e: e.dma_start(out=bcol_sb[:], in_=beta_col), "dls")
        t_bc = p.op("vector", lambda e: e.tensor_copy(bcol_bf[:], bcol_sb[:]),
                    deps=[t_bc0] + bar, inc="sv")
        t_bfr = p.dma("sync", lambda e: e.dma_start(
            out=bfca2_row[0:1, :].bitcast(F32), in_=bfca_r[0:1, :].bitcast(BF16) if False
            else bfca_r[0:1, :]), "dls") if False else None
        bmm = []
        for dci in range(DC):
            for h in range(2):
                bmm.append(p.op("tensor", lambda e, dci=dci, h=h: e.matmul(
                    PS[0:1, 3072 + h * 512: 3072 + (h + 1) * 512],
                    bcol_bf[:, dci:dci + 1],
                    wfca_sb[:, dci * D + h * 512: dci * D + (h + 1) * 512],
                    start=(dci == 0), stop=(dci == DC - 1)),
                    deps=[t_wfca, t_bc] + bar,
                    inc="pe" if (dci == DC - 1 and h == 1) else None))
        zb = p.op("vector", lambda e: e.memset(bfca2_row[:, :], 0.0), deps=bar, inc="sv")
        t_bfca = p.dma("sync", lambda e: e.dma_start(
            out=bfrow_f32[0:1, :], in_=bfca_r[0:1, :]), "dls", deps=bar)
        b2 = p.op("vector", lambda e: e.scalar_tensor_tensor(
            bfca2_row[0:1, :], PS[0:1, 3072:3072 + D], 1.0, bfrow_f32[0:1, :],
            op0=Op.mult, op1=Op.add), deps=[bmm[-1], t_bfca, zb], inc="sv")
        gsc = []
        for dci in range(DC):
            gsc.append(p.op("vector", lambda e, dci=dci: e.tensor_scalar(
                wfca_sb[:, dci * D:(dci + 1) * D], wfca_sb[:, dci * D:(dci + 1) * D],
                gcol_sb[:, dci:dci + 1], None, op0=Op.mult),
                deps=[b2, t_gc], inc="sv"))
        wfcg = gsc[-1]

        p4_hist = {}
        a7_hist = {}
        evz_hist = {}
        out_total = 0
        NW4 = NT // W4
        for w in range(NW4):
            us = [w * W4 + i for i in range(W4)]
            st = lnw4[w % 2]
            ldg = {}
            ldx = {}
            for i, u in enumerate(us):
                war = [p4_hist.get((w - 1, i))] if (w - 1, i) in p4_hist else bar
                ldg[i] = p.dma("gpsimd", lambda e, u=u, i=i: e.dma_start(
                    out=gt[i][:, :], in_=sp_g.ap()[u]), "dlg", deps=war)
                p.dma("gpsimd", lambda e, u=u, i=i: e.dma_start(
                    out=vt4[i][:, :], in_=sp_vq.ap()[u]), "dlg")
                ldx[i] = p.dma("sync", lambda e, u=u, i=i: e.dma_start(
                    out=xt4[i], in_=x_q[u]), "dls", deps=war)
            sls = []
            for i, u in enumerate(us):
                sls.append(p.op("scalar", lambda e, i=i: e.activation(
                    gt[i][:, :], gt[i][:, :], AF.Silu), deps=[ldg[i]], inc="ss"))
            a1s = []
            for i, u in enumerate(us):
                war = [p4_hist.get((w - 1, i))] if (w - 1, i) in p4_hist else bar
                a1s.append(p.op("vector", lambda e, u=u, i=i: e.reciprocal(
                    rl4[:, i:i + 1], acc[:, u * 1025 + 1024: u * 1025 + 1025]),
                    deps=[merge_tok[u]] + war, inc="sv"))
            a2s = []
            for i, u in enumerate(us):
                dep_extra = [b2] if (w == 0 and i == 0) else []
                if w > 0:
                    dep_extra.append(("dout", p.count["dout"]))
                a2s.append(p.op("scalar", lambda e, u=u, i=i: e.activation(
                    t4A[i], acc[:, u * 1025: u * 1025 + 1024], AF.Identity,
                    scale=rl4[:, i:i + 1]), deps=[a1s[i]] + dep_extra, inc="ss"))
            a3s = []
            for i, u in enumerate(us):
                a3s.append(p.op("vector", lambda e, i=i: e.tensor_tensor(
                    t4B[i], t4A[i], vt4[i][:, :], op=Op.add),
                    deps=[a2s[i], ldg[i]], inc="sv"))
            zn_toks = emit_ln_wave(p, [t4B[i] for i in range(W4)],
                                   [[zt[i]] for i in range(W4)], st,
                                   [[a3s[i]] for i in range(W4)],
                                   war=[p4_hist.get((w - 1, W4 - 1))] if w else bar)
            for i, u in enumerate(us):
                tts = []
                for dci in range(DC):
                    deps = [zn_toks[i], t_id, p4_hist.get((w - 1, i)),
                            evz_hist.get(i % 2)]
                    tts.append(p.op("tensor", lambda e, i=i, dci=dci: e.transpose(
                        PSbf[:, (i % 2) * 1024 + dci * 128:
                             (i % 2) * 1024 + (dci + 1) * 128],
                        zt[i][:, dci * 128:(dci + 1) * 128], ident_bf[:]),
                        deps=deps, inc="pe"))
                evz = p.op("scalar", lambda e, i=i: e.activation(
                    zT[i][:, 0:DC * 128],
                    PSbf[:, (i % 2) * 1024:(i % 2) * 1024 + DC * 128], AF.Identity),
                    deps=[tts[-1]], inc="ss")
                evz_hist[i % 2] = evz
                base = 1024 + (i % 2) * 1024
                mms = []
                for dci in range(DC):
                    for h in range(2):
                        mms.append(p.op("tensor", lambda e, i=i, dci=dci, h=h, base=base: e.matmul(
                            PS[:, base + h * 512: base + (h + 1) * 512],
                            zT[i][:, dci * 128:(dci + 1) * 128],
                            wfca_sb[:, dci * D + h * 512: dci * D + (h + 1) * 512],
                            start=(dci == 0), stop=False),
                            deps=[evz, wfcg, a7_hist.get(i % 2)], inc=None))
                for h in range(2):
                    mms.append(p.op("tensor", lambda e, h=h, base=base: e.matmul(
                        PS[:, base + h * 512: base + (h + 1) * 512], onesrow[:],
                        bfca2_row[:, h * 512:(h + 1) * 512],
                        start=False, stop=True), deps=[b2, orow],
                        inc="pe" if h == 1 else None))
                a7 = p.op("vector", lambda e, i=i, base=base: e.scalar_tensor_tensor(
                    t4A[i], PS[:, base:base + D], 1.0, gt[i][:, :],
                    op0=Op.mult, op1=Op.mult), deps=[mms[-1], sls[i]], inc="sv")
                a7_hist[i % 2] = a7
                a8 = p.op("vector", lambda e, i=i: e.tensor_tensor(
                    t4A[i], t4A[i], xt4[i], op=Op.add), deps=[a7, ldx[i]], inc="sv")
                p.dma("sync", lambda e, u=u, i=i: e.dma_start(
                    out=out_d[u], in_=t4A[i]), "dout", deps=[a8])
                out_total += 16
                p4_hist[(w, i)] = a8

        p.op("gpsimd", lambda e: e.nop(), deps=[("dout", out_total)])

        block = ST.enter_context(nc.Block())
        p.replay(block, sems)

    return nc



def make_masks(c):
    import ml_dtypes
    j = np.arange(128)[:, None]
    i = np.arange(128)[None, :]
    m = np.zeros((8, 128, 128), dtype=np.float32)
    for r in range(8):
        m[r] = ((j - i) <= 128 * (c - r)).astype(np.float32)
    return m.astype(ml_dtypes.bfloat16)


def kernel(L0_attn_x, L0_attn_Wq, L0_attn_Wk, L0_attn_Wv, L0_attn_Wfc, L0_attn_bfc,
           L0_attn_gamma, L0_attn_beta, L0_ffn_w_fc, L0_ffn_b_fc, L0_ffn_w_proj,
           L0_ffn_b_proj):
    import ml_dtypes
    x = np.ascontiguousarray(np.asarray(L0_attn_x, dtype=np.float32)[0])
    seq = x.shape[0]
    NT = seq // 128 // NCORES
    PROD = seq // NCORES
    WSH = HID // NCORES
    xt = x.reshape(NT, NCORES, 128, D)
    wfc = np.ascontiguousarray(np.asarray(L0_ffn_w_fc, dtype=np.float32))
    wproj = np.ascontiguousarray(np.asarray(L0_ffn_w_proj, dtype=np.float32))

    def rep(v):
        v = np.asarray(v, dtype=np.float32)
        return np.ascontiguousarray(np.broadcast_to(v[None, :], (128, v.shape[0])))

    ident_bf = np.eye(128, dtype=np.float32).astype(ml_dtypes.bfloat16)
    nc = build_nc(seq)
    in_maps = []
    for c in range(NCORES):
        in_maps.append({
            "x_q": np.ascontiguousarray(xt[:, c]),
            "x_p": np.ascontiguousarray(x[PROD * c: PROD * (c + 1)]),
            "wq": np.ascontiguousarray(np.asarray(L0_attn_Wq, dtype=np.float32)),
            "wk": np.ascontiguousarray(np.asarray(L0_attn_Wk, dtype=np.float32)),
            "wv": np.ascontiguousarray(np.asarray(L0_attn_Wv, dtype=np.float32)),
            "wfca": np.ascontiguousarray(np.asarray(L0_attn_Wfc, dtype=np.float32)),
            "wfc_sh": np.ascontiguousarray(wfc[WSH * c: WSH * (c + 1), :]),
            "wproj_sh": np.ascontiguousarray(wproj[:, WSH * c: WSH * (c + 1)]),
            "bfc_r": rep(L0_ffn_b_fc),
            "bproj_r": rep(L0_ffn_b_proj),
            "bfca_r": rep(L0_attn_bfc),
            "gamma_r": rep(L0_attn_gamma),
            "beta_r": rep(L0_attn_beta),
            "ident_bf": ident_bf,
            "masks": make_masks(c),
            "gamma_col": np.ascontiguousarray(
                np.asarray(L0_attn_gamma, dtype=np.float32).reshape(DC, 128).T),
            "beta_col": np.ascontiguousarray(
                np.asarray(L0_attn_beta, dtype=np.float32).reshape(DC, 128).T),
        })
    res = run_bass_kernel_spmd(nc, in_maps, core_ids=list(range(NCORES)))
    out = np.empty((NT, NCORES, 128, D), dtype=np.float32)
    for c in range(NCORES):
        out[:, c] = res.results[c]["out"]
    return out.reshape(1, seq, D)


# version 20 (final)
# speedup vs baseline: 1.1069x; 1.1069x over previous
import numpy as np
import concourse.bass as bass
import concourse.mybir as mybir
from concourse.bass_utils import run_bass_kernel_spmd
from concourse.alu_op_type import AluOpType as Op

F32 = mybir.dt.float32
BF16 = mybir.dt.bfloat16
AF = mybir.ActivationFunctionType
AX = mybir.AxisListType.X

LN2 = float(np.log(2.0))
LN2_INV = float(1.0 / np.log(2.0))
M32 = float(3 * 2.0 ** 22)
KBIG = float(3 * 2.0 ** 81)
EPS_FLAG = float(2.0 ** -30)

NCORES = 8
D = 1024
HID = 4096
DC = D // 128
HC = HID // 128

ENGINES = ("sync", "gpsimd", "vector", "scalar", "tensor")
ALLSEMS = (["dls", "dlg", "dout", "cc", "sv", "ss", "sg", "pe"] +
           [f"ds{i}" for i in range(4)] + [f"dv{i}" for i in range(4)] +
           [f"dq{i}" for i in range(4)] + ["dx0", "dx1", "dw0", "dw1",
                                           "dk0", "dk1"])


class Prog:
    def __init__(self):
        self.ops = {e: [] for e in ENGINES}
        self.count = {s: 0 for s in ALLSEMS}
        self.waited = {e: {} for e in ENGINES}

    DMA_SEMS = tuple(["dls", "dlg", "dout"] + [f"ds{i}" for i in range(4)] +
                     [f"dv{i}" for i in range(4)] + [f"dq{i}" for i in range(4)] +
                     ["dx0", "dx1", "dw0", "dw1", "dk0", "dk1"])

    def op(self, eng, fn, deps=(), inc=None, amt=1):
        waits = []
        for d in deps:
            if d is None:
                continue
            s, v = d
            if s in self.DMA_SEMS:
                v = self.count[s]
            if v > self.waited[eng].get(s, 0):
                waits.append((s, v))
                self.waited[eng][s] = v
        tok = None
        if inc is not None:
            self.count[inc] += amt
            tok = (inc, self.count[inc])
        self.ops[eng].append((waits, fn, inc, amt))
        return tok

    def dma(self, eng, fn, sem, deps=()):
        return self.op(eng, fn, deps, inc=sem, amt=16)

    def barrier(self):
        return [(s, self.count[s]) for s in
                ("sv", "ss", "sg", "pe") + self.DMA_SEMS]

    def replay(self, block, sems):
        def make(eng_name):
            def body(e):
                for waits, fn, inc, amt in self.ops[eng_name]:
                    for s, v in waits:
                        e.wait_ge(sems[s], v)
                    inst = fn(e)
                    if inc is not None:
                        assert inst is not None, f"inc={inc} op returned no inst"
                        inst.then_inc(sems[inc], amt)
            return body
        for e in ENGINES:
            if self.ops[e]:
                getattr(block, e)(make(e))


class Layout:
    def __init__(self, ap_f32, n_f32):
        self.ap = ap_f32
        self.apbf = ap_f32.bitcast(BF16)
        self.n = n_f32
        self.off = 0

    def reset(self):
        self.off = 0

    def f32(self, n):
        o = self.off
        assert o + n <= self.n, f"slab overflow f32 {o}+{n}>{self.n}"
        self.off = o + n
        return self.ap[:, o:o + n]

    def bf16(self, n):
        n_f = (n + 1) // 2
        o = self.off
        assert o + n_f <= self.n, f"slab overflow bf16 {o}+{n_f}>{self.n}"
        self.off = o + n_f
        return self.apbf[:, 2 * o:2 * o + n]


def emit_magic(p, t):
    m1 = p.op("vector", lambda e: e.memset(t["ones"][:], 1.0), inc="sv")
    m2 = p.op("vector", lambda e: e.memset(t["kb"][:], KBIG), inc="sv")
    a = p.op("vector", lambda e: e.tensor_scalar(
        t["flagc"][:], t["ones"][:], EPS_FLAG, 1.0, op0=Op.add, op1=Op.subtract),
        deps=[m1, m2], inc="sv")
    b = p.op("vector", lambda e: e.tensor_tensor(
        t["t1m"][:], t["kb"][:], t["flagc"][:], op=Op.mult), deps=[a], inc="sv")
    c = p.op("vector", lambda e: e.tensor_scalar(
        t["magic"][:], t["t1m"][:], M32, None, op0=Op.add), deps=[b], inc="sv")
    return p.op("vector", lambda e: e.tensor_scalar(
        t["nmagic"][:], t["magic"][:], -1.0, None, op0=Op.mult), deps=[c], inc="sv")


def emit_quant(p, v_ap, q_ap, nblk, s, magic_ap, magic_tok, vdep, war=()):
    toks = emit_quant_scales(p, v_ap, nblk, s, magic_ap, magic_tok, vdep, war)
    return emit_quant_apply(p, v_ap, q_ap, 0, nblk, s, s["tmp1"], magic_ap, toks)


def emit_quant_scales(p, v_ap, nblk, s, magic_ap, magic_tok, vdep, war=()):
    dd = list(vdep) + list(war) + [magic_tok]
    t0 = p.op("vector", lambda e: e.tensor_reduce(
        s["amax"][:, :nblk], v_ap.rearrange("p (b k) -> p b k", k=32),
        axis=AX, op=Op.max, apply_absolute_value=True), deps=dd, inc="sv")
    return emit_quant_scales_pre(p, nblk, s, magic_ap, [t0])


def emit_quant_scales_pre(p, nblk, s, magic_ap, amax_deps):
    nb = nblk
    dd = list(amax_deps)
    t1 = p.op("vector", lambda e: e.tensor_scalar(
        s["a2"][:, :nb], s["amax"][:, :nb], 2.0 ** -12, None, op0=Op.max), deps=dd, inc="sv")
    t2 = p.op("vector", lambda e: e.memset(s["pp"][:, :nb], 2.0 ** -12), deps=dd, inc="sv")
    cur = [t1, t2]
    bufa, bufb = "pp", "p2"
    for k in (16, 8, 4, 2, 1):
        a = p.op("vector", lambda e, k=k, bufa=bufa: e.tensor_scalar(
            s["p1"][:, :nb], s[bufa][:, :nb], float(2.0 ** k), None, op0=Op.mult),
            deps=cur, inc="sv")
        b = p.op("vector", lambda e: e.tensor_tensor(
            s["m"][:, :nb], s["a2"][:, :nb], s["p1"][:, :nb], op=Op.is_ge),
            deps=[a], inc="sv")
        c = p.op("vector", lambda e, k=k: e.tensor_scalar(
            s["f1"][:, :nb], s["m"][:, :nb], float(2.0 ** k - 1.0), 1.0,
            op0=Op.mult, op1=Op.add), deps=[b], inc="sv")
        d = p.op("vector", lambda e, bufa=bufa, bufb=bufb: e.tensor_tensor(
            s[bufb][:, :nb], s[bufa][:, :nb], s["f1"][:, :nb], op=Op.mult),
            deps=[c], inc="sv")
        cur = [d]
        bufa, bufb = bufb, bufa
    tb = p.op("vector", lambda e, bufa=bufa: e.tensor_scalar(
        s["scl"][:, :nb], s[bufa][:, :nb], 2.0 ** -3, None, op0=Op.mult),
        deps=cur, inc="sv")
    tc = p.op("vector", lambda e, bufa=bufa: e.reciprocal(s["rc"][:, :nb], s[bufa][:, :nb]),
              deps=cur, inc="sv")
    te = p.op("vector", lambda e: e.tensor_scalar(
        s["inv"][:, :nb], s["rc"][:, :nb], 2.0 ** 3, None, op0=Op.mult),
        deps=[tc], inc="sv")
    return [te, tb]


def emit_quant_apply(p, v_ap, q_ap, boff, nb, s, tmp1, magic_ap, deps):
    vB = v_ap.rearrange("p (b k) -> p k b", k=32)
    t1B = tmp1.rearrange("p (b k) -> p k b", k=32)
    invB = s["inv"][:, boff:boff + nb].unsqueeze(1).broadcast_to([128, 32, nb])
    sclB = s["scl"][:, boff:boff + nb].unsqueeze(1).broadcast_to([128, 32, nb])
    tf = p.op("vector", lambda e: e.scalar_tensor_tensor(
        t1B, vB, 1.0, invB, op0=Op.mult, op1=Op.mult), deps=deps, inc="sv")
    tg = p.op("vector", lambda e: e.tensor_scalar(
        v_ap, tmp1, magic_ap, magic_ap,
        op0=Op.add, op1=Op.subtract), deps=[tf], inc="sv")
    t2B = v_ap.rearrange("p (b k) -> p k b", k=32)
    qB = q_ap.rearrange("p (b k) -> p k b", k=32)
    th = p.op("vector", lambda e: e.scalar_tensor_tensor(
        qB, t2B, 1.0, sclB, op0=Op.mult, op1=Op.mult), deps=[tg], inc="sv")
    return th


def emit_quant_apply_x(p, v_ap, q_ap, boff, nb, s, tmp1, xbuf, magic_ap, deps,
                       nmagic_ap=None):
    invX = s["inv"][:, boff:boff + nb].unsqueeze(-1).broadcast_to([128, nb, 32])
    sclX = s["scl"][:, boff:boff + nb].unsqueeze(-1).broadcast_to([128, nb, 32])
    x3 = xbuf.rearrange("p (b k) -> p b k", k=32)
    e1 = p.op("vector", lambda e: e.tensor_copy(x3, invX), deps=deps, inc="sv")
    tf = p.op("vector", lambda e: e.scalar_tensor_tensor(
        tmp1, v_ap, 1.0, xbuf, op0=Op.mult, op1=Op.mult), deps=[e1], inc="sv")
    e2 = p.op("vector", lambda e: e.tensor_copy(x3, sclX), deps=[tf], inc="sv")
    if nmagic_ap is not None:
        tg1 = p.op("scalar", lambda e: e.activation(
            v_ap, tmp1, AF.Identity, bias=magic_ap), deps=[tf], inc="ss")
        tg = p.op("scalar", lambda e: e.activation(
            v_ap, v_ap, AF.Identity, bias=nmagic_ap), deps=[tg1], inc="ss")
    else:
        tg = p.op("vector", lambda e: e.tensor_scalar(
            v_ap, tmp1, magic_ap, magic_ap, op0=Op.add, op1=Op.subtract),
            deps=[tf], inc="sv")
    th = p.op("vector", lambda e: e.scalar_tensor_tensor(
        q_ap, v_ap, 1.0, xbuf, op0=Op.mult, op1=Op.mult), deps=[tg, e2], inc="sv")
    return th


def emit_ln_wave(p, xs, outs, st, deps_list, war=()):
    W = len(xs)
    rtoks = []
    stoks = []
    for i, x in enumerate(xs):
        rtoks.append(p.op("vector", lambda e, x=x, i=i: e.tensor_reduce(
            st["srow"][:, i:i + 1], x, axis=AX, op=Op.add, negate=True),
            deps=list(deps_list[i]) + list(war), inc="sv"))
        two = st.get("dump2") is not None
        back = 2 if two else 1
        dw = [stoks[i - back]] if i >= back else list(war)
        stoks.append(p.op("scalar", lambda e, x=x, i=i, two=two: e.activation(
            st["dump2" if (two and i % 2 == 1) else "dump"], x, AF.Square,
            accum_out=st["ssq"][:, i:i + 1]),
            deps=list(deps_list[i]) + dw, inc="ss"))
    a = p.op("vector", lambda e: e.tensor_scalar(
        st["nmean"][:, :W], st["srow"][:, :W], 1.0 / D, None, op0=Op.mult),
        deps=[rtoks[-1]], inc="sv")
    b = p.op("vector", lambda e: e.tensor_scalar(
        st["t1"][:, :W], st["ssq"][:, :W], 1.0 / D, None, op0=Op.mult),
        deps=[stoks[-1]], inc="sv")
    c = p.op("vector", lambda e: e.tensor_tensor(
        st["sq2"][:, :W], st["nmean"][:, :W], st["nmean"][:, :W], op=Op.mult),
        deps=[a], inc="sv")
    d = p.op("vector", lambda e: e.tensor_tensor(
        st["var"][:, :W], st["t1"][:, :W], st["sq2"][:, :W], op=Op.subtract),
        deps=[b, c], inc="sv")
    f = p.op("scalar", lambda e: e.activation(
        st["lnv"][:, :W], st["var"][:, :W], AF.Ln, bias=st["epsv"][:]),
        deps=[d], inc="ss")
    g = p.op("scalar", lambda e: e.activation(
        st["rstd"][:, :W], st["lnv"][:, :W], AF.Exp, scale=-0.5), deps=[f], inc="ss")
    h = p.op("vector", lambda e: e.tensor_tensor(
        st["bias2"][:, :W], st["nmean"][:, :W], st["rstd"][:, :W], op=Op.mult),
        deps=[g], inc="sv")
    for i, x in enumerate(xs):
        tok = None
        for o_ap in outs[i]:
            tok = p.op("scalar", lambda e, o=o_ap, x=x, i=i: e.activation(
                o, x, AF.Identity, scale=st["rstd"][:, i:i + 1],
                bias=st["bias2"][:, i:i + 1]), deps=[h], inc="ss")
        rtoks[i] = tok
    return rtoks


def emit_layernorm(p, x_ap, outs, t, xdep, war=()):
    dd = list(xdep) + list(war)
    a = p.op("vector", lambda e: e.tensor_reduce(
        t["srow"][:], x_ap, axis=AX, op=Op.add, negate=True), deps=dd, inc="sv")
    b = p.op("vector", lambda e: e.tensor_scalar(
        t["nmean"][:], t["srow"][:], 1.0 / D, None, op0=Op.mult), deps=[a], inc="sv")
    c = p.op("scalar", lambda e: e.activation(
        t["dump"], x_ap, AF.Square, accum_out=t["ssq"][:]), deps=dd, inc="ss")
    d = p.op("vector", lambda e: e.tensor_scalar(
        t["t1"][:], t["ssq"][:], 1.0 / D, None, op0=Op.mult), deps=[c], inc="sv")
    f = p.op("vector", lambda e: e.tensor_tensor(
        t["sq2"][:], t["nmean"][:], t["nmean"][:], op=Op.mult), deps=[b], inc="sv")
    g = p.op("vector", lambda e: e.tensor_tensor(
        t["var"][:], t["t1"][:], t["sq2"][:], op=Op.subtract), deps=[d, f], inc="sv")
    h = p.op("scalar", lambda e: e.activation(
        t["std"][:], t["var"][:], AF.Ln, bias=t["epsv"][:]), deps=[g], inc="ss")
    i = p.op("scalar", lambda e: e.activation(
        t["rstd"][:], t["std"][:], AF.Exp, scale=-0.5), deps=[h], inc="ss")
    j = p.op("vector", lambda e: e.tensor_tensor(
        t["bias2"][:], t["nmean"][:], t["rstd"][:], op=Op.mult), deps=[i], inc="sv")
    toks = []
    for o_ap in outs:
        toks.append(p.op("scalar", lambda e, o=o_ap: e.activation(
            o, x_ap, AF.Identity, scale=t["rstd"][:], bias=t["bias2"][:]),
            deps=[j], inc="ss"))
    return toks


def build_nc(seq):
    NT = seq // 128 // NCORES
    PROD = seq // NCORES
    PT = PROD // 128
    NKT = seq // 128
    NMS = NKT // 8
    WSH = HID // NCORES
    WT = WSH // 128

    nc = bass.Bass("TRN2", target_bir_lowering=False, detect_race_conditions=True)
    dt = nc.dram_tensor
    x_q = dt("x_q", [NT, 128, D], F32, kind="ExternalInput").ap()
    x_p = dt("x_p", [PROD, D], F32, kind="ExternalInput").ap()
    wq_d = dt("wq", [D, D], F32, kind="ExternalInput").ap()
    wk_d = dt("wk", [D, D], F32, kind="ExternalInput").ap()
    wv_d = dt("wv", [D, D], F32, kind="ExternalInput").ap()
    wfca_d = dt("wfca", [D, D], F32, kind="ExternalInput").ap()
    wfc_sh = dt("wfc_sh", [WSH, D], F32, kind="ExternalInput").ap()
    wproj_sh = dt("wproj_sh", [D, WSH], F32, kind="ExternalInput").ap()
    bfc_r = dt("bfc_r", [128, HID], F32, kind="ExternalInput").ap()
    bproj_r = dt("bproj_r", [128, D], F32, kind="ExternalInput").ap()
    bfca_r = dt("bfca_r", [128, D], F32, kind="ExternalInput").ap()
    gamma_r = dt("gamma_r", [128, D], F32, kind="ExternalInput").ap()
    beta_r = dt("beta_r", [128, D], F32, kind="ExternalInput").ap()
    ident_bf_d = dt("ident_bf", [128, 128], BF16, kind="ExternalInput").ap()
    masks_d = dt("masks", [8, 128, 128], BF16, kind="ExternalInput").ap()
    gamma_col = dt("gamma_col", [128, DC], F32, kind="ExternalInput").ap()
    beta_col = dt("beta_col", [128, DC], F32, kind="ExternalInput").ap()
    out_d = dt("out", [NT, 128, D], F32, kind="ExternalOutput").ap()

    cc_wi = dt("cc_wi", [2 * D * WSH], BF16)
    cc_wg = dt("cc_wg", [NCORES, 2 * D * WSH], BF16)
    cc_kvi = dt("cc_kvi", [2 * D * PROD], BF16)
    cc_kvg = dt("cc_kvg", [NCORES, 2 * D * PROD], BF16)
    wfc_in_v = cc_wi.ap()[0:D * WSH].rearrange("(a b) -> a b", b=WSH)
    wpj_in_v = cc_wi.ap()[D * WSH:2 * D * WSH].rearrange("(a b) -> a b", b=D)
    kt_in_v = cc_kvi.ap()[0:D * PROD].rearrange("(a b) -> a b", b=PROD)
    v_in_v = cc_kvi.ap()[D * PROD:2 * D * PROD].rearrange("(a b) -> a b", b=D)

    def wfc_g(g):
        return cc_wg.ap()[g][0:D * WSH].rearrange("(a b) -> a b", b=WSH)

    def wpj_g(g):
        return cc_wg.ap()[g][D * WSH:2 * D * WSH].rearrange("(a b) -> a b", b=D)

    def kt_g(r):
        return cc_kvg.ap()[r][0:D * PROD].rearrange("(a b) -> a b", b=PROD)

    def v_g(r):
        return cc_kvg.ap()[r][D * PROD:2 * D * PROD].rearrange("(a b) -> a b", b=D)
    sp_qt = dt("sp_qt", [D, PROD], BF16)
    sp_xnqt = dt("sp_xnqt", [NT, 128, DC * 128], BF16)
    sp_vq = dt("sp_vq", [NT, 128, D], BF16)
    sp_g = dt("sp_g", [NT, 128, D], BF16)

    from contextlib import ExitStack
    with ExitStack() as ST:
        def sbuf(name, shape, dtp=F32):
            return ST.enter_context(nc.sbuf_tensor(name, shape, dtp))

        tiny = {k: sbuf("m_" + k, [128, 1]) for k in ["ones", "kb", "flagc", "t1m", "magic", "nmagic"]}
        ident_bf = sbuf("identb", [128, 128], BF16)
        masks = sbuf("masks_s", [128, 8 * 128], BF16)
        onecol = sbuf("onecol", [128, 1], BF16)
        onesrow = sbuf("onesrow", [32, 128], BF16)
        gcol_sb = sbuf("gcol_sb", [128, DC])
        bcol_sb = sbuf("bcol_sb", [128, DC])
        bcol_bf = sbuf("bcol_bf", [128, DC], BF16)
        rl = sbuf("rl", [128, 1])
        lnt = {k: sbuf("ln_" + k, [128, 1]) for k in
               ["srow", "nmean", "t1", "sq2", "var", "std", "rstd", "bias2", "ssq", "epsv"]}
        lndump = sbuf("ln_dump", [128, D])
        qsc0 = {k: sbuf("qs_" + k, [128, 128]) for k in
                ["amax", "a2", "cf1", "cf2", "c", "p0", "p1", "p2", "pp", "m",
                 "f1", "f2", "rc", "scl", "inv"]}

        SLA_N = 25088
        SLB_N = 23552
        slabA = sbuf("slabA", [128, SLA_N])
        slabB = sbuf("slabB", [128, SLB_N])
        psum = ST.enter_context(nc.psum_tensor("ps", [128, 4096], F32))

        sems = {}
        for s in ALLSEMS:
            sems[s] = ST.enter_context(nc.semaphore(s))

        p = Prog()
        LA = Layout(slabA.ap(), SLA_N)
        LB = Layout(slabB.ap(), SLB_N)
        PS = psum.ap()
        PSbf = PS.bitcast(BF16)

        mg_tok = emit_magic(p, tiny)
        magic_ap = tiny["magic"][:]
        nmagic_ap = tiny["nmagic"][:]
        t_id = p.dma("sync", lambda e: e.dma_start(out=ident_bf[:], in_=ident_bf_d), "dls")
        t_mask = p.dma("sync", lambda e: e.dma_start(
            out=masks.ap().rearrange("p (r q) -> p r q", r=8), in_=masks_d.rearrange("r p q -> p r q")), "dls")
        oc = p.op("vector", lambda e: e.memset(onecol[:], 1.0), inc="sv")
        p.op("vector", lambda e: e.memset(lnt["epsv"][:], 1e-5), inc="sv")
        onecol_tok = oc

        LA.reset(); LB.reset()
        w_bf = {w: LA.bf16(DC * D) for w in ["wq", "wk", "wv"]}
        wdeps = {}
        for wname, dram in [("wq", wq_d), ("wk", wk_d), ("wv", wv_d)]:
            wdeps[wname] = p.dma("gpsimd", lambda e, w=wname, dr=dram: e.dma_start(
                out=w_bf[w].rearrange("p (dc k) -> p dc k", dc=DC),
                in_=dr.rearrange("(dc p) k -> p dc k", p=128)), "dlg")
        a_w32 = LA.f32(WT * D)
        a_wq = LA.bf16(WT * D)
        a_p32 = LA.f32(DC * WSH)
        a_pq = LA.bf16(DC * WSH)
        a_tmp = LB.f32(D)
        a_xbuf = LB.f32(D)
        a_stage = [LB.bf16(DC * 128) for _ in range(2)]

        t_w = p.dma("sync", lambda e: e.dma_start(
            out=a_w32.rearrange("p (t d) -> p t d", t=WT),
            in_=wfc_sh.rearrange("(t p) d -> p t d", p=128)), "dls")
        t_p = p.dma("sync", lambda e: e.dma_start(
            out=a_p32.rearrange("p (t w) -> p t w", t=DC),
            in_=wproj_sh.rearrange("(t p) w -> p t w", p=128)), "dls")

        NBW = D // 32
        NBP = WSH // 32
        am = []
        for i in range(WT):
            am.append(p.op("vector", lambda e, i=i: e.tensor_reduce(
                qsc0["amax"][:, i * NBW:(i + 1) * NBW],
                a_w32[:, i * D:(i + 1) * D].rearrange("p (b k) -> p b k", k=32),
                axis=AX, op=Op.max, apply_absolute_value=True),
                deps=[t_w, mg_tok], inc="sv"))
        tsc = emit_quant_scales_pre(p, WT * NBW, qsc0, magic_ap, am)
        tq_toks = []
        last_q = None
        for i in range(WT):
            tok = emit_quant_apply_x(p, a_w32[:, i * D:(i + 1) * D],
                                     a_wq[:, i * D:(i + 1) * D], i * NBW, NBW,
                                     qsc0, a_tmp, a_xbuf, magic_ap,
                                     tsc + ([last_q] if last_q else []))
            tq_toks.append(tok); last_q = tok
        am = []
        for i in range(DC):
            am.append(p.op("vector", lambda e, i=i: e.tensor_reduce(
                qsc0["amax"][:, i * NBP:(i + 1) * NBP],
                a_p32[:, i * WSH:(i + 1) * WSH].rearrange("p (b k) -> p b k", k=32),
                axis=AX, op=Op.max, apply_absolute_value=True),
                deps=[t_p, last_q], inc="sv"))
        tsc = emit_quant_scales_pre(p, DC * NBP, qsc0, magic_ap, am)
        tp_toks = []
        for i in range(DC):
            tok = emit_quant_apply_x(p, a_p32[:, i * WSH:(i + 1) * WSH],
                                     a_pq[:, i * WSH:(i + 1) * WSH], i * NBP, NBP,
                                     qsc0, a_tmp[:, 0:WSH], a_xbuf[:, 0:WSH],
                                     magic_ap, tsc + [last_q])
            tp_toks.append(tok); last_q = tok

        ev_war = None
        dma_war = {}
        for i in range(WT):
            sg = a_stage[i % 2]
            tts = []
            for dci in range(DC):
                deps = [tq_toks[i], t_id] + ([ev_war] if ev_war else [])
                tts.append(p.op("tensor", lambda e, i=i, dci=dci: e.transpose(
                    PSbf[:, (i % 2) * 1024 + dci * 128: (i % 2) * 1024 + (dci + 1) * 128],
                    a_wq[:, i * D + dci * 128: i * D + (dci + 1) * 128],
                    ident_bf[:]), deps=deps, inc="pe"))
            ev = p.op("scalar", lambda e, i=i, sg=sg: e.activation(
                sg[:, 0:DC * 128], PSbf[:, (i % 2) * 1024:(i % 2) * 1024 + DC * 128],
                AF.Identity),
                deps=[tts[-1]] + ([dma_war.get(i - 2)] if (i - 2) in dma_war else []),
                inc="ss")
            ev_war = ev
            dma_war[i] = p.dma("gpsimd", lambda e, i=i, sg=sg: e.dma_start(
                out=wfc_in_v.rearrange("(dc q) h -> q dc h", q=128)[:, :, i * 128:(i + 1) * 128],
                in_=sg.rearrange("p (dc t) -> p dc t", dc=DC)), "dlg", deps=[ev])
        for i in range(DC):
            j = WT + i
            sg = a_stage[j % 2]
            tts = []
            for hci in range(WT):
                deps = [tp_toks[i], t_id] + ([ev_war] if ev_war else [])
                tts.append(p.op("tensor", lambda e, i=i, hci=hci, j=j: e.transpose(
                    PSbf[:, (j % 2) * 1024 + hci * 128: (j % 2) * 1024 + (hci + 1) * 128],
                    a_pq[:, i * WSH + hci * 128: i * WSH + (hci + 1) * 128],
                    ident_bf[:]), deps=deps, inc="pe"))
            ev = p.op("scalar", lambda e, i=i, j=j, sg=sg: e.activation(
                sg[:, 0:WSH], PSbf[:, (j % 2) * 1024:(j % 2) * 1024 + WSH], AF.Identity),
                deps=[tts[-1]] + ([dma_war.get(j - 2)] if (j - 2) in dma_war else []),
                inc="ss")
            ev_war = ev
            dma_war[j] = p.dma("gpsimd", lambda e, i=i, sg=sg: e.dma_start(
                out=wpj_in_v.rearrange("(hc q) dd -> q hc dd", q=128)[:, :, i * 128:(i + 1) * 128],
                in_=sg[:, 0:WSH].rearrange("p (hc t) -> p hc t", hc=WT)), "dlg", deps=[ev])

        dlg_now = ("dlg", p.count["dlg"])

        bar = p.barrier()
        LA.reset(); LB.reset()
        LA.f32(3 * (DC * D // 2))
        WV = min(4, PT)
        xa = [LA.f32(D) for _ in range(WV)]
        xnf = [LA.f32(D) for _ in range(WV)]
        xnbf = [LA.bf16(D) for _ in range(WV)]
        xnq = [LB.bf16(D) for _ in range(WV)]
        xnT = LB.bf16(DC * PROD)
        stg = [LB.bf16(512) for _ in range(4)]
        stg2 = [LB.bf16(D) for _ in range(4)]
        q_tmp = LB.f32(D)
        q_xbuf = LB.f32(D)
        lndump2 = LB.f32(D)
        xnqt_stage = [LB.bf16(DC * 128) for _ in range(4)]
        xa2 = [LB.f32(D) for _ in range(WV)]
        lnw = [dict({k: sbuf(f"lw{i}_{k}", [128, WV]) for k in
                     ["srow", "nmean", "t1", "sq2", "var", "lnv", "rstd", "bias2", "ssq"]},
                    dump=lndump.ap(), dump2=lndump2, epsv=lnt["epsv"]) for i in range(2)]

        xp_t = x_p.rearrange("(t p) d -> t p d", p=128)
        NW = PT // WV
        WTOK = 128 * WV

        qtoks = {}
        pstate = {"stg_dma": {}, "stg2_dma": {}, "xq_dma": {}, "kcev": None,
                  "vev": None, "xqev": None}

        def emit_projchunk(wname, dest_cc, toff, ntok, dep_xnT):
            st_ = pstate
            for kc in range(DC):
                sg = stg[kc % 4]
                sem = f"ds{kc % 4}"
                bank = (kc % 2) * 512
                mms = []
                for dci in range(DC):
                    deps = [dep_xnT, wdeps[wname]] +                         ([st_["kcev"]] if st_["kcev"] else [])
                    mms.append(p.op("tensor", lambda e, kc=kc, dci=dci, bank=bank,
                                    toff=toff, ntok=ntok: e.matmul(
                        PS[:, bank:bank + ntok],
                        w_bf[wname][:, dci * D + kc * 128: dci * D + (kc + 1) * 128],
                        xnT[:, dci * PROD + toff: dci * PROD + toff + ntok],
                        start=(dci == 0), stop=(dci == DC - 1)),
                        deps=deps, inc="pe" if dci == DC - 1 else None))
                ev = p.op("scalar", lambda e, sg=sg, bank=bank, ntok=ntok: e.activation(
                    sg[:, 0:ntok], PS[:, bank:bank + ntok], AF.Identity),
                    deps=[mms[-1], st_["stg_dma"].get(kc % 4)], inc="ss")
                st_["kcev"] = ev
                st_["stg_dma"][kc % 4] = p.dma("gpsimd", lambda e, kc=kc, sg=sg,
                                               toff=toff, ntok=ntok: e.dma_start(
                    out=dest_cc[kc * 128:(kc + 1) * 128, toff:toff + ntok],
                    in_=sg[:, 0:ntok]), sem, deps=[ev])

        def emit_vtile(tt, dest_fn, dep_xnT):
            st_ = pstate
            sg = stg2[tt % 4]
            sem = f"dv{tt % 4}"
            base = 2048 + (tt % 2) * 1024
            mms = []
            for dci in range(DC):
                for h in range(2):
                    deps = [dep_xnT, wdeps["wv"]] + ([st_["vev"]] if st_["vev"] else [])
                    mms.append(p.op("tensor", lambda e, tt=tt, dci=dci, h=h, base=base: e.matmul(
                        PS[:, base + h * 512: base + (h + 1) * 512],
                        xnT[:, dci * PROD + tt * 128: dci * PROD + (tt + 1) * 128],
                        w_bf["wv"][:, dci * D + h * 512: dci * D + (h + 1) * 512],
                        start=(dci == 0), stop=(dci == DC - 1)),
                        deps=deps, inc="pe" if (dci == DC - 1 and h == 1) else None))
            ev = p.op("scalar", lambda e, sg=sg, base=base: e.activation(
                sg[:, 0:D], PS[:, base:base + D], AF.Identity),
                deps=[mms[-1], st_["stg2_dma"].get(tt % 4)], inc="ss")
            st_["vev"] = ev
            st_["stg2_dma"][tt % 4] = dest_fn(tt, sg, ev, sem)

        def emit_rows(which, entry_war=()):
            ln_prev = {}
            ev_war = None
            q_last_l = None
            for w in range(NW):
                tiles = [w * WV + i for i in range(WV)]
                st = lnw[w % 2]
                xw = xa if w % 2 == 0 else xa2
                ldeps = []
                for i, tt in enumerate(tiles):
                    if (w - 2) in ln_prev:
                        war = [ln_prev.get(w - 2)]
                    elif which == "prod":
                        war = [tq_toks[-1]] if w % 2 == 0 else []
                    else:
                        war = [prod_ln]
                    xsem = f"dx{w % 2}"
                    if which == "prod":
                        tok = p.dma("sync", lambda e, tt=tt, i=i, xw=xw: e.dma_start(
                            out=xw[i], in_=xp_t[tt]), xsem, deps=war)
                    else:
                        tok = p.dma("sync", lambda e, tt=tt, i=i, xw=xw: e.dma_start(
                            out=xw[i], in_=x_q[tt]), xsem, deps=war)
                    ldeps.append([tok])
                outs = [[xnbf[i]] if which == "prod" else [xnbf[i], xnf[i]]
                        for i in range(WV)]
                lt = emit_ln_wave(p, [xw[i] for i in range(WV)], outs, st, ldeps,
                                  war=([ln_prev.get(w - 1)] if (w - 1) in ln_prev
                                       else bar + list(entry_war)))
                ln_prev[w] = lt[-1]
                if which == "query":
                    am = []
                    NBX = D // 32
                    for i in range(WV):
                        am.append(p.op("vector", lambda e, i=i: e.tensor_reduce(
                            qsc0["amax"][:, i * NBX:(i + 1) * NBX],
                            xnf[i].rearrange("p (b k) -> p b k", k=32),
                            axis=AX, op=Op.max, apply_absolute_value=True),
                            deps=[lt[i], mg_tok] + ([q_last_l] if q_last_l else []),
                            inc="sv"))
                    tsc = emit_quant_scales_pre(p, WV * NBX, qsc0, magic_ap, am)
                    for i in range(WV):
                        q_last_l = emit_quant_apply_x(
                            p, xnf[i], xnq[i], i * NBX, NBX, qsc0, q_tmp,
                            q_xbuf, magic_ap,
                            tsc + ([q_last_l] if q_last_l else []))
                        qtoks[tiles[i]] = q_last_l
                for i, tt in enumerate(tiles):
                    tts = []
                    for dci in range(DC):
                        deps = [lt[i], t_id] + ([ev_war] if ev_war else []) +                             list(entry_war)
                        tts.append(p.op("tensor", lambda e, i=i, dci=dci: e.transpose(
                            PSbf[:, 2048 + dci * 128: 2048 + (dci + 1) * 128],
                            xnbf[i][:, dci * 128:(dci + 1) * 128], ident_bf[:]),
                            deps=deps, inc="pe"))
                    ev_war = p.op("scalar", lambda e, tt=tt: e.activation(
                        xnT.rearrange("p (dc t2) -> p dc t2", dc=DC)[:, :, tt * 128:(tt + 1) * 128],
                        PSbf[:, 2048:2048 + DC * 128].rearrange("p (dc t2) -> p dc t2", dc=DC),
                        AF.Identity), deps=[tts[-1]] + list(entry_war), inc="ss")
                    if which == "query":
                        sgx = xnqt_stage[tt % 4]
                        sem = f"dq{tt % 4}"
                        tts2 = []
                        for dci in range(DC):
                            deps = [qtoks[tt], t_id] +                                 ([pstate["xqev"]] if pstate["xqev"] else [])
                            tts2.append(p.op("tensor", lambda e, i=i, dci=dci: e.transpose(
                                PSbf[:, 3072 + dci * 128: 3072 + (dci + 1) * 128],
                                xnq[i][:, dci * 128:(dci + 1) * 128], ident_bf[:]),
                                deps=deps, inc="pe"))
                        ev2 = p.op("scalar", lambda e, sgx=sgx: e.activation(
                            sgx[:, 0:DC * 128], PSbf[:, 3072:3072 + DC * 128],
                            AF.Identity),
                            deps=[tts2[-1], pstate["xq_dma"].get(tt % 4)],
                            inc="ss")
                        pstate["xqev"] = ev2
                        pstate["xq_dma"][tt % 4] = p.dma(
                            "gpsimd", lambda e, tt=tt, sgx=sgx: e.dma_start(
                                out=sp_xnqt.ap()[tt], in_=sgx[:, 0:DC * 128]), sem,
                            deps=[ev2])
                xnT_wave = ev_war
                if which == "prod":
                    for i, tt in enumerate(tiles):
                        emit_vtile(tt, lambda tt, sg, ev, sem: p.dma(
                            "gpsimd", lambda e, tt=tt, sg=sg: e.dma_start(
                                out=v_in_v[tt * 128:(tt + 1) * 128, :],
                                in_=sg[:, 0:D]), sem, deps=[ev]), xnT_wave)
                    emit_projchunk("wk", kt_in_v, w * WTOK, WTOK, xnT_wave)
                else:
                    for i, tt in enumerate(tiles):
                        emit_vtile(tt, lambda tt, sg, ev, sem: p.dma(
                            "gpsimd", lambda e, tt=tt, sg=sg: e.dma_start(
                                out=sp_vq.ap()[tt], in_=sg[:, 0:D]), sem,
                            deps=[ev]), xnT_wave)
                    emit_projchunk("wq", sp_qt.ap(), w * WTOK, WTOK, xnT_wave)
            return ev_war, ln_prev[NW - 1]

        xnT_done, prod_ln = emit_rows("prod")
        ktv_sp = [(f"ds{i}", p.count[f"ds{i}"]) for i in range(4)] + \
                 [(f"dv{i}", p.count[f"dv{i}"]) for i in range(4)]
        t_ccw = p.op("gpsimd", lambda e: e.collective_compute(
            "AllGather", Op.bypass, replica_groups=[list(range(NCORES))],
            ins=[cc_wi.ap()], outs=[cc_wg.ap()]), deps=[dlg_now], inc="cc")
        t_ccp = t_ccw

        kv_pe = ("pe", p.count["pe"])
        _, _ = emit_rows("query", entry_war=[kv_pe, prod_ln])
        q_last = qtoks[NT - 1]
        xnqt_dma = [pstate["xq_dma"][i] for i in pstate["xq_dma"]]

        bar = p.barrier()
        LA.reset(); LB.reset()
        wfcqT_sb = LA.bf16(DC * HID)
        hb2 = [LA.f32(HID // 2) for _ in range(2)]
        ffn_tmp1 = LA.f32(HID // 2)
        bfc_row = LA.bf16(HID)[0:32, :]
        bproj_row = LA.bf16(D)[0:32, :]
        wprojqT_sb = LB.bf16(HC * D)
        hq = [LB.bf16(HID) for _ in range(2)]
        ffn_xbuf = LB.f32(HID // 2)
        hqT = hq
        xnqt_t = [LB.bf16(DC * 128) for _ in range(2)]

        wf4 = wfcqT_sb.rearrange("p (dc g h) -> p dc g h", dc=DC, g=NCORES)
        wp4 = wprojqT_sb.rearrange("p (g x dd) -> p g x dd", g=NCORES, x=WT)
        def load_xnqt(u, deps):
            pp = u % 2
            return p.dma("gpsimd", lambda e, u=u, pp=pp: e.dma_start(
                out=xnqt_t[pp][:, :], in_=sp_xnqt.ap()[u]), "dlg", deps=deps)
        xnqt_ld = [None, None]
        xnqt_ld[0] = load_xnqt(0, bar + [q_last])
        xnqt_ld[1] = load_xnqt(1, bar + [q_last])
        wf_toks = []
        for g in range(NCORES):
            wf_toks.append(p.dma("gpsimd", lambda e, g=g: e.dma_start(
                out=wf4[:, :, g, :],
                in_=wfc_g(g).rearrange("(dc p) h -> p dc h", p=128)), "dlg",
                deps=[t_ccw] + bar))
        for g in range(NCORES):
            t_wp = p.dma("sync", lambda e, g=g: e.dma_start(
                out=wp4[:, g, :, :],
                in_=wpj_g(g).rearrange("(x p) dd -> p x dd", p=128)), "dls",
                deps=[t_ccp] + bar)
        z1 = p.op("vector", lambda e: e.memset(bfc_row[:, :], 0.0), deps=bar, inc="sv")
        z2 = p.op("vector", lambda e: e.memset(bproj_row[:, :], 0.0), deps=bar, inc="sv")
        t_bf = p.dma("gpsimd", lambda e: e.dma_start(out=bfc_row[0:1, :], in_=bfc_r[0:1, :]),
                     "dlg", deps=bar + [z1, z2])
        t_bp = p.dma("gpsimd", lambda e: e.dma_start(out=bproj_row[0:1, :], in_=bproj_r[0:1, :]),
                     "dlg")
        z3 = p.op("vector", lambda e: e.memset(onesrow[:, :], 0.0), deps=bar, inc="sv")
        orow = p.op("vector", lambda e: e.memset(onesrow[0:1, :], 1.0), deps=[z3], inc="sv")
        t_cckv = p.op("gpsimd", lambda e: e.collective_compute(
            "AllGather", Op.bypass, replica_groups=[list(range(NCORES))],
            ins=[cc_kvi.ap()], outs=[cc_kvg.ap()]), deps=ktv_sp, inc="cc")
        t_cckt = t_ccv = t_cckv

        evac_toks = {}
        apply_half = {}
        apply_tok = {}
        hqT_tok = {}
        g_toks = []
        projread_tok = {}
        fc_pe_tok = {}

        FCBANK = (0, 512, 1536, 2048)

        def emit_fc(u):
            pp = u % 2
            deps0 = bar + [q_last, t_bf, orow, xnqt_ld[pp]]
            for h in range(2):
                k = 2 * u + h
                hbuf = hb2[k % 2]
                evs = []
                for g in range(4):
                    gg = 4 * h + g
                    bank = FCBANK[g]
                    war = [evac_toks[k - 1][g]] if (k - 1) in evac_toks else []
                    war = war + [wf_toks[gg]]
                    mms = []
                    for dci in range(DC):
                        mms.append(p.op("tensor", lambda e, gg=gg, dci=dci, bank=bank, pp=pp: e.matmul(
                            PS[:, bank:bank + 512],
                            xnqt_t[pp][:, dci * 128:(dci + 1) * 128],
                            wfcqT_sb[:, dci * HID + gg * 512: dci * HID + (gg + 1) * 512],
                            start=(dci == 0), stop=False),
                            deps=deps0 + war, inc=None))
                    mms.append(p.op("tensor", lambda e, gg=gg, bank=bank: e.matmul(
                        PS[:, bank:bank + 512], onesrow[:],
                        bfc_row[:, gg * 512:(gg + 1) * 512],
                        start=False, stop=True), deps=deps0 + war, inc="pe"))
                    hbwar = [apply_half[k - 2]] if (k - 2) in apply_half else []
                    evs.append(p.op("scalar", lambda e, g=g, bank=bank, hbuf=hbuf: e.activation(
                        hbuf[:, g * 512:(g + 1) * 512], PS[:, bank:bank + 512],
                        AF.Gelu), deps=[mms[-1]] + hbwar, inc="ss"))
                fc_pe_tok[u] = mms[-1]
                evac_toks[k] = evs

        def emit_mid(u):
            NBH = (HID // 2) // 32
            ams = []
            for h in range(2):
                k = 2 * u + h
                hbuf = hb2[k % 2]
                war = [apply_half[2 * u - 1]] if (2 * u - 1) in apply_half else []
                ams.append(p.op("vector", lambda e, hbuf=hbuf, h=h: e.tensor_reduce(
                    qsc0["amax"][:, h * NBH:(h + 1) * NBH],
                    hbuf.rearrange("p (b kk) -> p b kk", kk=32),
                    axis=AX, op=Op.max, apply_absolute_value=True),
                    deps=list(evac_toks[k]) + [mg_tok] + war, inc="sv"))
            tsc = emit_quant_scales_pre(p, 2 * NBH, qsc0, magic_ap, ams)
            war = []
            if (u - 2) in projread_tok:
                war.append(projread_tok[u - 2])
            prev = []
            for h in range(2):
                k = 2 * u + h
                hbuf = hb2[k % 2]
                apply_half[k] = emit_quant_apply_x(
                    p, hbuf, hq[u % 2][:, h * (HID // 2):(h + 1) * (HID // 2)],
                    h * NBH, NBH, qsc0, ffn_tmp1, ffn_xbuf, magic_ap,
                    tsc + war + prev)
                prev = [apply_half[k]]
            apply_tok[u] = apply_half[2 * u + 1]

        def emit_tail(u):
            hqu = hq[u % 2]
            ev_list = []
            for grp in range(HC // 8):
                tb = 2048 if grp % 2 == 0 else 5120
                tts = []
                for j in range(8):
                    hci = grp * 8 + j
                    deps = [apply_tok[u], t_id]
                    if grp >= 2:
                        deps.append(ev_list[grp - 2])
                    elif (u - 1) in hqT_tok:
                        deps.append(hqT_tok[u - 1])
                    elif (u - 2) in hqT_tok:
                        deps.append(hqT_tok[u - 2])
                    tts.append(p.op("tensor", lambda e, hci=hci, j=j, hqu=hqu, tb=tb: e.transpose(
                        PSbf[:, tb + j * 128: tb + (j + 1) * 128],
                        hqu[:, hci * 128:(hci + 1) * 128], ident_bf[:]),
                        deps=deps, inc="pe"))
                ev_list.append(p.op("scalar", lambda e, grp=grp, hqu=hqu, tb=tb: e.activation(
                    hqu[:, grp * 1024:(grp + 1) * 1024],
                    PSbf[:, tb:tb + 1024], AF.Identity), deps=[tts[-1]], inc="ss"))
            hqT_tok[u] = ev_list[-1]
            proj_mms = []
            war = [projread_tok[u - 1]] if (u - 1) in projread_tok else []
            for hci in range(HC):
                for h in range(2):
                    deps = [ev_list[-1], t_wp, t_bp] + war
                    proj_mms.append(p.op("tensor", lambda e, hci=hci, h=h, hqu=hqu: e.matmul(
                        PS[:, 3072 + h * 512: 3072 + (h + 1) * 512],
                        hqu[:, hci * 128:(hci + 1) * 128],
                        wprojqT_sb[:, hci * D + h * 512: hci * D + (h + 1) * 512],
                        start=(hci == 0), stop=False), deps=deps, inc=None))
            for h in range(2):
                proj_mms.append(p.op("tensor", lambda e, h=h: e.matmul(
                    PS[:, 3072 + h * 512: 3072 + (h + 1) * 512], onesrow[:],
                    bproj_row[:, h * 512:(h + 1) * 512],
                    start=False, stop=True), deps=[t_bp] + war,
                    inc="pe" if h == 1 else None))
            t_mm = proj_mms[-1]
            gout = xnqt_t[u % 2]
            t_g = p.op("vector", lambda e, gout=gout: e.tensor_copy(
                gout, PS[:, 3072:3072 + D]),
                deps=[t_mm, fc_pe_tok[u]], inc="sv")
            projread_tok[u] = t_g
            g_toks.append(p.dma("gpsimd", lambda e, u=u, gout=gout: e.dma_start(
                out=sp_g.ap()[u], in_=gout), "dlg", deps=[t_g]))

        for u in range(NT):
            pp = u % 2
            if u >= 2:
                war_x = [fc_pe_tok[u - 2]] if (u - 2) in fc_pe_tok else []
                if u - 4 >= 0:
                    war_x.append(g_toks[u - 4])
                xnqt_ld[pp] = load_xnqt(u, bar + [q_last] + war_x)
            emit_fc(u)
            if u >= 2:
                emit_tail(u - 2)
            emit_mid(u)
        emit_tail(NT - 2)
        emit_tail(NT - 1)
        t_g = g_toks[-1]

        bar = p.barrier()
        LA.reset(); LB.reset()
        CKT = min(4, PROD // 128)
        CW = 128 * CKT
        CPM0 = 8 // CKT
        NBK = 2 * CPM0
        NBV = 2 * CPM0 - 1
        NCH = NKT // CKT
        QTs = LA.bf16(DC * PROD)
        acc = LA.f32(NT * 1025)
        ktb = [LB.bf16(DC * CW) for _ in range(NBK)]
        vtb = [LB.bf16(CKT * D) for _ in range(NBV)]
        expT = LB.bf16(8 * PROD)
        tmpb = LB.bf16(128)

        qt_sp = [(f"ds{i}", p.count[f"ds{i}"]) for i in range(4)]
        t_qt = p.dma("sync", lambda e: e.dma_start(
            out=QTs.rearrange("p (kc t) -> p kc t", kc=DC),
            in_=sp_qt.ap().rearrange("(kc p) t -> p kc t", p=128)), "dls",
            deps=qt_sp + [fc_pe_tok[NT - 1]])
        accz = p.op("vector", lambda e: e.memset(acc, 0.0), deps=bar, inc="sv")

        def load_k(q, deps):
            r = (CW * q) // PROD
            l0 = (CW * q) % PROD
            return p.dma("sync", lambda e, r=r, l0=l0, q=q: e.dma_start(
                out=ktb[q % NBK].rearrange("p (dc w) -> p dc w", dc=DC),
                in_=kt_g(r)[:, l0:l0 + CW].rearrange("(dc p) w -> p dc w", p=128)),
                f"dk{(q // CPM0) % 2}", deps=deps + [t_cckt])

        def load_v(q, deps):
            r = (CW * q) // PROD
            l0 = (CW * q) % PROD
            return p.dma("gpsimd", lambda e, r=r, l0=l0, q=q: e.dma_start(
                out=vtb[q % NBV].rearrange("p (s dd) -> p s dd", s=CKT),
                in_=v_g(r)[l0:l0 + CW, :].rearrange("(s p) dd -> p s dd", p=128)),
                f"dw{(q // CPM0) % 2}", deps=deps + [t_ccv])

        CPM = CPM0
        ktoks = {}
        vtoks = {}
        for q in range(min(2 * CPM, NCH)):
            ktoks[q] = load_k(q, bar)
        for q in range(min(NBV, NCH)):
            vtoks[q] = load_v(q, bar)
        chunk_pe = {}
        attv_pe = {}
        exp_toks = {}
        tm_prev = [None]
        merge_tok = [accz] * NT
        slot_war = [None, None]
        sc_war = [None, None]
        for ms in range(NMS):
            for kt in range(8 * ms, 8 * ms + 8):
                q = kt // CKT
                tk = ktoks[q]
                ub = kt // 8
                qoff = 128 * ub
                chunks = []
                pos = qoff
                while pos < PROD:
                    ln = min(512, PROD - pos)
                    chunks.append((pos, ln))
                    pos += ln
                for ci, (qs, ln) in enumerate(chunks):
                    bank = (ci % 2) * 512
                    mms = []
                    for dci in range(DC):
                        deps = [tk, t_qt] + ([sc_war[ci % 2]] if sc_war[ci % 2] else [])
                        mms.append(p.op(
                            "tensor",
                            lambda e, kt=kt, dci=dci, qs=qs, ln=ln, bank=bank: e.matmul(
                                PS[:, bank:bank + ln],
                                ktb[(kt // CKT) % NBK][:, dci * CW + (kt % CKT) * 128:
                                                       dci * CW + (kt % CKT) * 128 + 128],
                                QTs[:, dci * PROD + qs: dci * PROD + qs + ln],
                                start=(dci == 0), stop=(dci == DC - 1)),
                            deps=deps, inc="pe" if dci == DC - 1 else None))
                    chunk_pe[q] = mms[-1]
                    col0 = (kt % 8) * PROD + qs
                    if ci == 0:
                        t_e0 = p.op("scalar", lambda e, bank=bank: e.activation(
                            tmpb[:, 0:128], PS[:, bank:bank + 128], AF.Exp,
                            scale=1.0 / 32.0), deps=[mms[-1], tm_prev[0]], inc="ss")
                        t_m = p.op("vector", lambda e, kt=kt, col0=col0: e.tensor_tensor(
                            expT[:, col0:col0 + 128], tmpb[:, 0:128],
                            masks[:, (kt % 8) * 128:(kt % 8) * 128 + 128], op=Op.mult),
                            deps=[t_e0, t_mask], inc="sv")
                        tm_prev[0] = t_m
                        ek = [t_m]
                        if ln > 128:
                            t_e1 = p.op("scalar",
                                        lambda e, bank=bank, ln=ln, col0=col0: e.activation(
                                            expT[:, col0 + 128: col0 + ln],
                                            PS[:, bank + 128:bank + ln], AF.Exp,
                                            scale=1.0 / 32.0), deps=[mms[-1]], inc="ss")
                            ek.append(t_e1)
                            sc_war[ci % 2] = t_e1
                        else:
                            sc_war[ci % 2] = t_e0
                        exp_toks[kt] = ek
                    else:
                        t_e = p.op("scalar",
                                   lambda e, bank=bank, ln=ln, col0=col0: e.activation(
                                       expT[:, col0: col0 + ln], PS[:, bank:bank + ln],
                                       AF.Exp, scale=1.0 / 32.0), deps=[mms[-1]], inc="ss")
                        sc_war[ci % 2] = t_e
                        exp_toks[kt].append(t_e)
            for q in range(CPM * (ms + 2), CPM * (ms + 3)):
                if q < NCH and q not in ktoks:
                    war = [chunk_pe.get(q - NBK)]
                    ktoks[q] = load_k(q, [w for w in war if w])
            for u in range(ms, NT):
                sl = u % 2
                base = 1024 + sl * 1536
                first = True
                t_mm = None
                for kt in range(8 * ms, 8 * ms + 8):
                    deps = list(exp_toks[kt]) + [vtoks[kt // CKT]]
                    if slot_war[sl] is not None:
                        deps.append(slot_war[sl])
                    ecol = (kt % 8) * PROD + u * 128
                    vt = vtb[(kt // CKT) % NBV]
                    for part in range(2):
                        p.op("tensor",
                             lambda e, kt=kt, part=part, base=base, ecol=ecol, vt=vt, first=first, ms=ms: e.matmul(
                                 PS[:, base + part * 512: base + (part + 1) * 512],
                                 expT[:, ecol:ecol + 128],
                                 vt[:, (kt % CKT) * D + part * 512:(kt % CKT) * D + (part + 1) * 512],
                                 start=first, stop=(kt == 8 * ms + 7)),
                             deps=deps, inc=None)
                    t_mm = p.op("tensor",
                                lambda e, kt=kt, base=base, ecol=ecol, first=first, ms=ms: e.matmul(
                                    PS[:, base + 1024: base + 1025],
                                    expT[:, ecol:ecol + 128], onecol[:],
                                    start=first, stop=(kt == 8 * ms + 7)),
                                deps=deps + [onecol_tok],
                                inc="pe" if kt == 8 * ms + 7 else None)
                    first = False
                mtk = p.op("vector", lambda e, u=u, base=base: e.tensor_tensor(
                    acc[:, u * 1025:(u + 1) * 1025], acc[:, u * 1025:(u + 1) * 1025],
                    PS[:, base:base + 1025], op=Op.add),
                    deps=[t_mm, merge_tok[u]], inc="sv")
                merge_tok[u] = mtk
                slot_war[sl] = mtk
                for q in range(CPM * ms, CPM * (ms + 1)):
                    attv_pe[q] = t_mm
            for q in range(CPM * (ms + 1), CPM * (ms + 3)):
                if q < NCH and q not in vtoks:
                    if q - NBV >= 0 and attv_pe.get(q - NBV) is None:
                        continue
                    war = [attv_pe.get(q - NBV)]
                    vtoks[q] = load_v(q, [w for w in war if w])


        bar = p.barrier()
        LB.reset()
        W4 = min(4, NT)
        wfca_sb = LB.bf16(DC * D)
        bfca2_row = LB.bf16(D)[0:32, :]
        t4A = [LB.f32(D) for _ in range(W4)]
        bfrow_f32 = t4A[0][0:1, :]
        t4B = [LB.f32(D) for _ in range(W4)]
        xt4 = [LB.f32(D) for _ in range(W4)]
        gt = [LB.bf16(D) for _ in range(W4)]
        vt4 = [LB.bf16(D) for _ in range(W4)]
        zt = vt4
        zT = [LB.bf16(DC * 128) for _ in range(W4)]
        rl4 = sbuf("rl4", [128, W4])
        lnw4 = [dict(d, dump2=None) for d in lnw]

        t_wfca = p.dma("gpsimd", lambda e: e.dma_start(
            out=wfca_sb.rearrange("p (dc k) -> p dc k", dc=DC),
            in_=wfca_d.rearrange("(dc p) k -> p dc k", p=128)), "dlg", deps=bar)
        t_gc = p.dma("sync", lambda e: e.dma_start(out=gcol_sb[:], in_=gamma_col), "dls",
                     deps=bar)
        t_bc0 = p.dma("sync", lambda e: e.dma_start(out=bcol_sb[:], in_=beta_col), "dls")
        t_bc = p.op("vector", lambda e: e.tensor_copy(bcol_bf[:], bcol_sb[:]),
                    deps=[t_bc0] + bar, inc="sv")
        t_bfr = p.dma("sync", lambda e: e.dma_start(
            out=bfca2_row[0:1, :].bitcast(F32), in_=bfca_r[0:1, :].bitcast(BF16) if False
            else bfca_r[0:1, :]), "dls") if False else None
        bmm = []
        for dci in range(DC):
            for h in range(2):
                bmm.append(p.op("tensor", lambda e, dci=dci, h=h: e.matmul(
                    PS[0:1, 3072 + h * 512: 3072 + (h + 1) * 512],
                    bcol_bf[:, dci:dci + 1],
                    wfca_sb[:, dci * D + h * 512: dci * D + (h + 1) * 512],
                    start=(dci == 0), stop=(dci == DC - 1)),
                    deps=[t_wfca, t_bc] + bar,
                    inc="pe" if (dci == DC - 1 and h == 1) else None))
        zb = p.op("vector", lambda e: e.memset(bfca2_row[:, :], 0.0), deps=bar, inc="sv")
        t_bfca = p.dma("sync", lambda e: e.dma_start(
            out=bfrow_f32[0:1, :], in_=bfca_r[0:1, :]), "dls", deps=bar)
        b2 = p.op("vector", lambda e: e.scalar_tensor_tensor(
            bfca2_row[0:1, :], PS[0:1, 3072:3072 + D], 1.0, bfrow_f32[0:1, :],
            op0=Op.mult, op1=Op.add), deps=[bmm[-1], t_bfca, zb], inc="sv")
        gsc = []
        for dci in range(DC):
            gsc.append(p.op("vector", lambda e, dci=dci: e.tensor_scalar(
                wfca_sb[:, dci * D:(dci + 1) * D], wfca_sb[:, dci * D:(dci + 1) * D],
                gcol_sb[:, dci:dci + 1], None, op0=Op.mult),
                deps=[b2, t_gc], inc="sv"))
        wfcg = gsc[-1]

        p4_hist = {}
        a7_hist = {}
        evz_hist = {}
        out_total = 0
        NW4 = NT // W4
        for w in range(NW4):
            us = [w * W4 + i for i in range(W4)]
            st = lnw4[w % 2]
            ldg = {}
            ldx = {}
            for i, u in enumerate(us):
                war = [p4_hist.get((w - 1, i))] if (w - 1, i) in p4_hist else bar
                ldg[i] = p.dma("gpsimd", lambda e, u=u, i=i: e.dma_start(
                    out=gt[i][:, :], in_=sp_g.ap()[u]), "dlg", deps=war)
                p.dma("gpsimd", lambda e, u=u, i=i: e.dma_start(
                    out=vt4[i][:, :], in_=sp_vq.ap()[u]), "dlg")
                ldx[i] = p.dma("sync", lambda e, u=u, i=i: e.dma_start(
                    out=xt4[i], in_=x_q[u]), "dls", deps=war)
            sls = []
            for i, u in enumerate(us):
                sls.append(p.op("scalar", lambda e, i=i: e.activation(
                    gt[i][:, :], gt[i][:, :], AF.Silu), deps=[ldg[i]], inc="ss"))
            a1s = []
            for i, u in enumerate(us):
                war = [p4_hist.get((w - 1, i))] if (w - 1, i) in p4_hist else bar
                a1s.append(p.op("vector", lambda e, u=u, i=i: e.reciprocal(
                    rl4[:, i:i + 1], acc[:, u * 1025 + 1024: u * 1025 + 1025]),
                    deps=[merge_tok[u]] + war, inc="sv"))
            a2s = []
            for i, u in enumerate(us):
                dep_extra = [b2] if (w == 0 and i == 0) else []
                if w > 0:
                    dep_extra.append(("dout", p.count["dout"]))
                a2s.append(p.op("scalar", lambda e, u=u, i=i: e.activation(
                    t4A[i], acc[:, u * 1025: u * 1025 + 1024], AF.Identity,
                    scale=rl4[:, i:i + 1]), deps=[a1s[i]] + dep_extra, inc="ss"))
            a3s = []
            for i, u in enumerate(us):
                a3s.append(p.op("vector", lambda e, i=i: e.tensor_tensor(
                    t4B[i], t4A[i], vt4[i][:, :], op=Op.add),
                    deps=[a2s[i], ldg[i]], inc="sv"))
            zn_toks = emit_ln_wave(p, [t4B[i] for i in range(W4)],
                                   [[zt[i]] for i in range(W4)], st,
                                   [[a3s[i]] for i in range(W4)],
                                   war=[p4_hist.get((w - 1, W4 - 1))] if w else bar)
            for i, u in enumerate(us):
                tts = []
                for dci in range(DC):
                    deps = [zn_toks[i], t_id, p4_hist.get((w - 1, i)),
                            evz_hist.get(i % 2)]
                    tts.append(p.op("tensor", lambda e, i=i, dci=dci: e.transpose(
                        PSbf[:, (i % 2) * 1024 + dci * 128:
                             (i % 2) * 1024 + (dci + 1) * 128],
                        zt[i][:, dci * 128:(dci + 1) * 128], ident_bf[:]),
                        deps=deps, inc="pe"))
                evz = p.op("scalar", lambda e, i=i: e.activation(
                    zT[i][:, 0:DC * 128],
                    PSbf[:, (i % 2) * 1024:(i % 2) * 1024 + DC * 128], AF.Identity),
                    deps=[tts[-1]], inc="ss")
                evz_hist[i % 2] = evz
                base = 1024 + (i % 2) * 1024
                mms = []
                for dci in range(DC):
                    for h in range(2):
                        mms.append(p.op("tensor", lambda e, i=i, dci=dci, h=h, base=base: e.matmul(
                            PS[:, base + h * 512: base + (h + 1) * 512],
                            zT[i][:, dci * 128:(dci + 1) * 128],
                            wfca_sb[:, dci * D + h * 512: dci * D + (h + 1) * 512],
                            start=(dci == 0), stop=False),
                            deps=[evz, wfcg, a7_hist.get(i % 2)], inc=None))
                for h in range(2):
                    mms.append(p.op("tensor", lambda e, h=h, base=base: e.matmul(
                        PS[:, base + h * 512: base + (h + 1) * 512], onesrow[:],
                        bfca2_row[:, h * 512:(h + 1) * 512],
                        start=False, stop=True), deps=[b2, orow],
                        inc="pe" if h == 1 else None))
                a7 = p.op("vector", lambda e, i=i, base=base: e.scalar_tensor_tensor(
                    t4A[i], PS[:, base:base + D], 1.0, gt[i][:, :],
                    op0=Op.mult, op1=Op.mult), deps=[mms[-1], sls[i]], inc="sv")
                a7_hist[i % 2] = a7
                a8 = p.op("vector", lambda e, i=i: e.tensor_tensor(
                    t4A[i], t4A[i], xt4[i], op=Op.add), deps=[a7, ldx[i]], inc="sv")
                p.dma("sync", lambda e, u=u, i=i: e.dma_start(
                    out=out_d[u], in_=t4A[i]), "dout", deps=[a8])
                out_total += 16
                p4_hist[(w, i)] = a8

        p.op("gpsimd", lambda e: e.nop(), deps=[("dout", out_total)])

        block = ST.enter_context(nc.Block())
        p.replay(block, sems)

    return nc



def make_masks(c):
    import ml_dtypes
    j = np.arange(128)[:, None]
    i = np.arange(128)[None, :]
    m = np.zeros((8, 128, 128), dtype=np.float32)
    for r in range(8):
        m[r] = ((j - i) <= 128 * (c - r)).astype(np.float32)
    return m.astype(ml_dtypes.bfloat16)


def kernel(L0_attn_x, L0_attn_Wq, L0_attn_Wk, L0_attn_Wv, L0_attn_Wfc, L0_attn_bfc,
           L0_attn_gamma, L0_attn_beta, L0_ffn_w_fc, L0_ffn_b_fc, L0_ffn_w_proj,
           L0_ffn_b_proj):
    import ml_dtypes
    x = np.ascontiguousarray(np.asarray(L0_attn_x, dtype=np.float32)[0])
    seq = x.shape[0]
    NT = seq // 128 // NCORES
    PROD = seq // NCORES
    WSH = HID // NCORES
    xt = x.reshape(NT, NCORES, 128, D)
    wfc = np.ascontiguousarray(np.asarray(L0_ffn_w_fc, dtype=np.float32))
    wproj = np.ascontiguousarray(np.asarray(L0_ffn_w_proj, dtype=np.float32))

    def rep(v):
        v = np.asarray(v, dtype=np.float32)
        return np.ascontiguousarray(np.broadcast_to(v[None, :], (128, v.shape[0])))

    ident_bf = np.eye(128, dtype=np.float32).astype(ml_dtypes.bfloat16)
    nc = build_nc(seq)
    in_maps = []
    for c in range(NCORES):
        in_maps.append({
            "x_q": np.ascontiguousarray(xt[:, c]),
            "x_p": np.ascontiguousarray(x[PROD * c: PROD * (c + 1)]),
            "wq": np.ascontiguousarray(np.asarray(L0_attn_Wq, dtype=np.float32)),
            "wk": np.ascontiguousarray(np.asarray(L0_attn_Wk, dtype=np.float32)),
            "wv": np.ascontiguousarray(np.asarray(L0_attn_Wv, dtype=np.float32)),
            "wfca": np.ascontiguousarray(np.asarray(L0_attn_Wfc, dtype=np.float32)),
            "wfc_sh": np.ascontiguousarray(wfc[WSH * c: WSH * (c + 1), :]),
            "wproj_sh": np.ascontiguousarray(wproj[:, WSH * c: WSH * (c + 1)]),
            "bfc_r": rep(L0_ffn_b_fc),
            "bproj_r": rep(L0_ffn_b_proj),
            "bfca_r": rep(L0_attn_bfc),
            "gamma_r": rep(L0_attn_gamma),
            "beta_r": rep(L0_attn_beta),
            "ident_bf": ident_bf,
            "masks": make_masks(c),
            "gamma_col": np.ascontiguousarray(
                np.asarray(L0_attn_gamma, dtype=np.float32).reshape(DC, 128).T),
            "beta_col": np.ascontiguousarray(
                np.asarray(L0_attn_beta, dtype=np.float32).reshape(DC, 128).T),
        })
    res = run_bass_kernel_spmd(nc, in_maps, core_ids=list(range(NCORES)))
    out = np.empty((NT, NCORES, 128, D), dtype=np.float32)
    for c in range(NCORES):
        out[:, c] = res.results[c]["out"]
    return out.reshape(1, seq, D)


# version 21
# speedup vs baseline: 1.1279x; 1.0189x over previous
import numpy as np
import concourse.bass as bass
import concourse.mybir as mybir
from concourse.bass_utils import run_bass_kernel_spmd
from concourse.alu_op_type import AluOpType as Op

F32 = mybir.dt.float32
BF16 = mybir.dt.bfloat16
AF = mybir.ActivationFunctionType
AX = mybir.AxisListType.X

LN2 = float(np.log(2.0))
LN2_INV = float(1.0 / np.log(2.0))
M32 = float(3 * 2.0 ** 22)
KBIG = float(3 * 2.0 ** 81)
EPS_FLAG = float(2.0 ** -30)

NCORES = 8
D = 1024
HID = 4096
DC = D // 128
HC = HID // 128

ENGINES = ("sync", "gpsimd", "vector", "scalar", "tensor")
ALLSEMS = (["dls", "dlg", "dout", "cc", "sv", "ss", "sg", "pe"] +
           [f"ds{i}" for i in range(4)] + [f"dv{i}" for i in range(4)] +
           [f"dq{i}" for i in range(4)] + ["dx0", "dx1", "dw0", "dw1",
                                           "dk0", "dk1"])


class Prog:
    def __init__(self):
        self.ops = {e: [] for e in ENGINES}
        self.count = {s: 0 for s in ALLSEMS}
        self.waited = {e: {} for e in ENGINES}

    DMA_SEMS = tuple(["dls", "dlg", "dout"] + [f"ds{i}" for i in range(4)] +
                     [f"dv{i}" for i in range(4)] + [f"dq{i}" for i in range(4)] +
                     ["dx0", "dx1", "dw0", "dw1", "dk0", "dk1"])

    def op(self, eng, fn, deps=(), inc=None, amt=1):
        waits = []
        for d in deps:
            if d is None:
                continue
            s, v = d
            if s in self.DMA_SEMS:
                v = self.count[s]
            if v > self.waited[eng].get(s, 0):
                waits.append((s, v))
                self.waited[eng][s] = v
        tok = None
        if inc is not None:
            self.count[inc] += amt
            tok = (inc, self.count[inc])
        self.ops[eng].append((waits, fn, inc, amt))
        return tok

    def dma(self, eng, fn, sem, deps=()):
        return self.op(eng, fn, deps, inc=sem, amt=16)

    def barrier(self):
        return [(s, self.count[s]) for s in
                ("sv", "ss", "sg", "pe") + self.DMA_SEMS]

    def replay(self, block, sems):
        def make(eng_name):
            def body(e):
                for waits, fn, inc, amt in self.ops[eng_name]:
                    for s, v in waits:
                        e.wait_ge(sems[s], v)
                    inst = fn(e)
                    if inc is not None:
                        assert inst is not None, f"inc={inc} op returned no inst"
                        inst.then_inc(sems[inc], amt)
            return body
        for e in ENGINES:
            if self.ops[e]:
                getattr(block, e)(make(e))


class Layout:
    def __init__(self, ap_f32, n_f32):
        self.ap = ap_f32
        self.apbf = ap_f32.bitcast(BF16)
        self.n = n_f32
        self.off = 0

    def reset(self):
        self.off = 0

    def f32(self, n):
        o = self.off
        assert o + n <= self.n, f"slab overflow f32 {o}+{n}>{self.n}"
        self.off = o + n
        return self.ap[:, o:o + n]

    def bf16(self, n):
        n_f = (n + 1) // 2
        o = self.off
        assert o + n_f <= self.n, f"slab overflow bf16 {o}+{n_f}>{self.n}"
        self.off = o + n_f
        return self.apbf[:, 2 * o:2 * o + n]


def emit_magic(p, t):
    m1 = p.op("vector", lambda e: e.memset(t["ones"][:], 1.0), inc="sv")
    m2 = p.op("vector", lambda e: e.memset(t["kb"][:], KBIG), inc="sv")
    a = p.op("vector", lambda e: e.tensor_scalar(
        t["flagc"][:], t["ones"][:], EPS_FLAG, 1.0, op0=Op.add, op1=Op.subtract),
        deps=[m1, m2], inc="sv")
    b = p.op("vector", lambda e: e.tensor_tensor(
        t["t1m"][:], t["kb"][:], t["flagc"][:], op=Op.mult), deps=[a], inc="sv")
    c = p.op("vector", lambda e: e.tensor_scalar(
        t["magic"][:], t["t1m"][:], M32, None, op0=Op.add), deps=[b], inc="sv")
    return p.op("vector", lambda e: e.tensor_scalar(
        t["nmagic"][:], t["magic"][:], -1.0, None, op0=Op.mult), deps=[c], inc="sv")


def emit_quant(p, v_ap, q_ap, nblk, s, magic_ap, magic_tok, vdep, war=()):
    toks = emit_quant_scales(p, v_ap, nblk, s, magic_ap, magic_tok, vdep, war)
    return emit_quant_apply(p, v_ap, q_ap, 0, nblk, s, s["tmp1"], magic_ap, toks)


def emit_quant_scales(p, v_ap, nblk, s, magic_ap, magic_tok, vdep, war=()):
    dd = list(vdep) + list(war) + [magic_tok]
    t0 = p.op("vector", lambda e: e.tensor_reduce(
        s["amax"][:, :nblk], v_ap.rearrange("p (b k) -> p b k", k=32),
        axis=AX, op=Op.max, apply_absolute_value=True), deps=dd, inc="sv")
    return emit_quant_scales_pre(p, nblk, s, magic_ap, [t0])


def emit_quant_scales_pre(p, nblk, s, magic_ap, amax_deps):
    nb = nblk
    dd = list(amax_deps)
    t1 = p.op("vector", lambda e: e.tensor_scalar(
        s["a2"][:, :nb], s["amax"][:, :nb], 2.0 ** -12, None, op0=Op.max), deps=dd, inc="sv")
    t2 = p.op("vector", lambda e: e.memset(s["pp"][:, :nb], 2.0 ** -12), deps=dd, inc="sv")
    cur = [t1, t2]
    bufa, bufb = "pp", "p2"
    for k in (16, 8, 4, 2, 1):
        a = p.op("vector", lambda e, k=k, bufa=bufa: e.tensor_scalar(
            s["p1"][:, :nb], s[bufa][:, :nb], float(2.0 ** k), None, op0=Op.mult),
            deps=cur, inc="sv")
        b = p.op("vector", lambda e: e.tensor_tensor(
            s["m"][:, :nb], s["a2"][:, :nb], s["p1"][:, :nb], op=Op.is_ge),
            deps=[a], inc="sv")
        c = p.op("vector", lambda e, k=k: e.tensor_scalar(
            s["f1"][:, :nb], s["m"][:, :nb], float(2.0 ** k - 1.0), 1.0,
            op0=Op.mult, op1=Op.add), deps=[b], inc="sv")
        d = p.op("vector", lambda e, bufa=bufa, bufb=bufb: e.tensor_tensor(
            s[bufb][:, :nb], s[bufa][:, :nb], s["f1"][:, :nb], op=Op.mult),
            deps=[c], inc="sv")
        cur = [d]
        bufa, bufb = bufb, bufa
    tb = p.op("vector", lambda e, bufa=bufa: e.tensor_scalar(
        s["scl"][:, :nb], s[bufa][:, :nb], 2.0 ** -3, None, op0=Op.mult),
        deps=cur, inc="sv")
    tc = p.op("vector", lambda e, bufa=bufa: e.reciprocal(s["rc"][:, :nb], s[bufa][:, :nb]),
              deps=cur, inc="sv")
    te = p.op("vector", lambda e: e.tensor_scalar(
        s["inv"][:, :nb], s["rc"][:, :nb], 2.0 ** 3, None, op0=Op.mult),
        deps=[tc], inc="sv")
    return [te, tb]


def emit_quant_apply(p, v_ap, q_ap, boff, nb, s, tmp1, magic_ap, deps):
    vB = v_ap.rearrange("p (b k) -> p k b", k=32)
    t1B = tmp1.rearrange("p (b k) -> p k b", k=32)
    invB = s["inv"][:, boff:boff + nb].unsqueeze(1).broadcast_to([128, 32, nb])
    sclB = s["scl"][:, boff:boff + nb].unsqueeze(1).broadcast_to([128, 32, nb])
    tf = p.op("vector", lambda e: e.scalar_tensor_tensor(
        t1B, vB, 1.0, invB, op0=Op.mult, op1=Op.mult), deps=deps, inc="sv")
    tg = p.op("vector", lambda e: e.tensor_scalar(
        v_ap, tmp1, magic_ap, magic_ap,
        op0=Op.add, op1=Op.subtract), deps=[tf], inc="sv")
    t2B = v_ap.rearrange("p (b k) -> p k b", k=32)
    qB = q_ap.rearrange("p (b k) -> p k b", k=32)
    th = p.op("vector", lambda e: e.scalar_tensor_tensor(
        qB, t2B, 1.0, sclB, op0=Op.mult, op1=Op.mult), deps=[tg], inc="sv")
    return th


def emit_quant_apply_x(p, v_ap, q_ap, boff, nb, s, tmp1, xbuf, magic_ap, deps,
                       nmagic_ap=None):
    invX = s["inv"][:, boff:boff + nb].unsqueeze(-1).broadcast_to([128, nb, 32])
    sclX = s["scl"][:, boff:boff + nb].unsqueeze(-1).broadcast_to([128, nb, 32])
    x3 = xbuf.rearrange("p (b k) -> p b k", k=32)
    e1 = p.op("vector", lambda e: e.tensor_copy(x3, invX), deps=deps, inc="sv")
    tf = p.op("vector", lambda e: e.scalar_tensor_tensor(
        tmp1, v_ap, 1.0, xbuf, op0=Op.mult, op1=Op.mult), deps=[e1], inc="sv")
    e2 = p.op("vector", lambda e: e.tensor_copy(x3, sclX), deps=[tf], inc="sv")
    if nmagic_ap is not None:
        tg1 = p.op("scalar", lambda e: e.activation(
            v_ap, tmp1, AF.Identity, bias=magic_ap), deps=[tf], inc="ss")
        tg = p.op("scalar", lambda e: e.activation(
            v_ap, v_ap, AF.Identity, bias=nmagic_ap), deps=[tg1], inc="ss")
    else:
        tg = p.op("vector", lambda e: e.tensor_scalar(
            v_ap, tmp1, magic_ap, magic_ap, op0=Op.add, op1=Op.subtract),
            deps=[tf], inc="sv")
    th = p.op("vector", lambda e: e.scalar_tensor_tensor(
        q_ap, v_ap, 1.0, xbuf, op0=Op.mult, op1=Op.mult), deps=[tg, e2], inc="sv")
    return th


def emit_ln_wave(p, xs, outs, st, deps_list, war=()):
    W = len(xs)
    rtoks = []
    stoks = []
    for i, x in enumerate(xs):
        rtoks.append(p.op("vector", lambda e, x=x, i=i: e.tensor_reduce(
            st["srow"][:, i:i + 1], x, axis=AX, op=Op.add, negate=True),
            deps=list(deps_list[i]) + list(war), inc="sv"))
        two = st.get("dump2") is not None
        back = 2 if two else 1
        dw = [stoks[i - back]] if i >= back else list(war)
        stoks.append(p.op("scalar", lambda e, x=x, i=i, two=two: e.activation(
            st["dump2" if (two and i % 2 == 1) else "dump"], x, AF.Square,
            accum_out=st["ssq"][:, i:i + 1]),
            deps=list(deps_list[i]) + dw, inc="ss"))
    a = p.op("vector", lambda e: e.tensor_scalar(
        st["nmean"][:, :W], st["srow"][:, :W], 1.0 / D, None, op0=Op.mult),
        deps=[rtoks[-1]], inc="sv")
    b = p.op("vector", lambda e: e.tensor_scalar(
        st["t1"][:, :W], st["ssq"][:, :W], 1.0 / D, None, op0=Op.mult),
        deps=[stoks[-1]], inc="sv")
    c = p.op("vector", lambda e: e.tensor_tensor(
        st["sq2"][:, :W], st["nmean"][:, :W], st["nmean"][:, :W], op=Op.mult),
        deps=[a], inc="sv")
    d = p.op("vector", lambda e: e.tensor_tensor(
        st["var"][:, :W], st["t1"][:, :W], st["sq2"][:, :W], op=Op.subtract),
        deps=[b, c], inc="sv")
    f = p.op("scalar", lambda e: e.activation(
        st["lnv"][:, :W], st["var"][:, :W], AF.Ln, bias=st["epsv"][:]),
        deps=[d], inc="ss")
    g = p.op("scalar", lambda e: e.activation(
        st["rstd"][:, :W], st["lnv"][:, :W], AF.Exp, scale=-0.5), deps=[f], inc="ss")
    h = p.op("vector", lambda e: e.tensor_tensor(
        st["bias2"][:, :W], st["nmean"][:, :W], st["rstd"][:, :W], op=Op.mult),
        deps=[g], inc="sv")
    for i, x in enumerate(xs):
        tok = None
        for o_ap in outs[i]:
            tok = p.op("scalar", lambda e, o=o_ap, x=x, i=i: e.activation(
                o, x, AF.Identity, scale=st["rstd"][:, i:i + 1],
                bias=st["bias2"][:, i:i + 1]), deps=[h], inc="ss")
        rtoks[i] = tok
    return rtoks


def emit_layernorm(p, x_ap, outs, t, xdep, war=()):
    dd = list(xdep) + list(war)
    a = p.op("vector", lambda e: e.tensor_reduce(
        t["srow"][:], x_ap, axis=AX, op=Op.add, negate=True), deps=dd, inc="sv")
    b = p.op("vector", lambda e: e.tensor_scalar(
        t["nmean"][:], t["srow"][:], 1.0 / D, None, op0=Op.mult), deps=[a], inc="sv")
    c = p.op("scalar", lambda e: e.activation(
        t["dump"], x_ap, AF.Square, accum_out=t["ssq"][:]), deps=dd, inc="ss")
    d = p.op("vector", lambda e: e.tensor_scalar(
        t["t1"][:], t["ssq"][:], 1.0 / D, None, op0=Op.mult), deps=[c], inc="sv")
    f = p.op("vector", lambda e: e.tensor_tensor(
        t["sq2"][:], t["nmean"][:], t["nmean"][:], op=Op.mult), deps=[b], inc="sv")
    g = p.op("vector", lambda e: e.tensor_tensor(
        t["var"][:], t["t1"][:], t["sq2"][:], op=Op.subtract), deps=[d, f], inc="sv")
    h = p.op("scalar", lambda e: e.activation(
        t["std"][:], t["var"][:], AF.Ln, bias=t["epsv"][:]), deps=[g], inc="ss")
    i = p.op("scalar", lambda e: e.activation(
        t["rstd"][:], t["std"][:], AF.Exp, scale=-0.5), deps=[h], inc="ss")
    j = p.op("vector", lambda e: e.tensor_tensor(
        t["bias2"][:], t["nmean"][:], t["rstd"][:], op=Op.mult), deps=[i], inc="sv")
    toks = []
    for o_ap in outs:
        toks.append(p.op("scalar", lambda e, o=o_ap: e.activation(
            o, x_ap, AF.Identity, scale=t["rstd"][:], bias=t["bias2"][:]),
            deps=[j], inc="ss"))
    return toks


def build_nc(seq):
    NT = seq // 128 // NCORES
    PROD = seq // NCORES
    PT = PROD // 128
    NKT = seq // 128
    NMS = NKT // 8
    WSH = HID // NCORES
    WT = WSH // 128

    nc = bass.Bass("TRN2", target_bir_lowering=False, detect_race_conditions=True)
    dt = nc.dram_tensor
    x_q = dt("x_q", [NT, 128, D], F32, kind="ExternalInput").ap()
    x_p = dt("x_p", [PROD, D], F32, kind="ExternalInput").ap()
    wq_d = dt("wq", [D, D], F32, kind="ExternalInput").ap()
    wk_d = dt("wk", [D, D], F32, kind="ExternalInput").ap()
    wv_d = dt("wv", [D, D], F32, kind="ExternalInput").ap()
    wfca_d = dt("wfca", [D, D], F32, kind="ExternalInput").ap()
    wfc_sh = dt("wfc_sh", [WSH, D], F32, kind="ExternalInput").ap()
    wproj_sh = dt("wproj_sh", [D, WSH], F32, kind="ExternalInput").ap()
    bfc_r = dt("bfc_r", [128, HID], F32, kind="ExternalInput").ap()
    bproj_r = dt("bproj_r", [128, D], F32, kind="ExternalInput").ap()
    bfca_r = dt("bfca_r", [128, D], F32, kind="ExternalInput").ap()
    gamma_r = dt("gamma_r", [128, D], F32, kind="ExternalInput").ap()
    beta_r = dt("beta_r", [128, D], F32, kind="ExternalInput").ap()
    ident_bf_d = dt("ident_bf", [128, 128], BF16, kind="ExternalInput").ap()
    masks_d = dt("masks", [8, 128, 128], BF16, kind="ExternalInput").ap()
    gamma_col = dt("gamma_col", [128, DC], F32, kind="ExternalInput").ap()
    beta_col = dt("beta_col", [128, DC], F32, kind="ExternalInput").ap()
    out_d = dt("out", [NT, 128, D], F32, kind="ExternalOutput").ap()

    cc_wi = dt("cc_wi", [2 * D * WSH], BF16)
    cc_wg = dt("cc_wg", [NCORES, 2 * D * WSH], BF16)
    cc_kvi = dt("cc_kvi", [2 * D * PROD], BF16)
    cc_kvg = dt("cc_kvg", [NCORES, 2 * D * PROD], BF16)
    wfc_in_v = cc_wi.ap()[0:D * WSH].rearrange("(a b) -> a b", b=WSH)
    wpj_in_v = cc_wi.ap()[D * WSH:2 * D * WSH].rearrange("(a b) -> a b", b=D)
    kt_in_v = cc_kvi.ap()[0:D * PROD].rearrange("(a b) -> a b", b=PROD)
    v_in_v = cc_kvi.ap()[D * PROD:2 * D * PROD].rearrange("(a b) -> a b", b=D)

    def wfc_g(g):
        return cc_wg.ap()[g][0:D * WSH].rearrange("(a b) -> a b", b=WSH)

    def wpj_g(g):
        return cc_wg.ap()[g][D * WSH:2 * D * WSH].rearrange("(a b) -> a b", b=D)

    def kt_g(r):
        return cc_kvg.ap()[r][0:D * PROD].rearrange("(a b) -> a b", b=PROD)

    def v_g(r):
        return cc_kvg.ap()[r][D * PROD:2 * D * PROD].rearrange("(a b) -> a b", b=D)
    sp_qt = dt("sp_qt", [D, PROD], BF16)
    sp_xnqt = dt("sp_xnqt", [NT, 128, DC * 128], BF16)
    sp_vq = dt("sp_vq", [NT, 128, D], BF16)
    sp_g = dt("sp_g", [NT, 128, D], BF16)

    from contextlib import ExitStack
    with ExitStack() as ST:
        def sbuf(name, shape, dtp=F32):
            return ST.enter_context(nc.sbuf_tensor(name, shape, dtp))

        tiny = {k: sbuf("m_" + k, [128, 1]) for k in ["ones", "kb", "flagc", "t1m", "magic", "nmagic"]}
        ident_bf = sbuf("identb", [128, 128], BF16)
        masks = sbuf("masks_s", [128, 8 * 128], BF16)
        onecol = sbuf("onecol", [128, 1], BF16)
        onesrow = sbuf("onesrow", [32, 128], BF16)
        gcol_sb = sbuf("gcol_sb", [128, DC])
        bcol_sb = sbuf("bcol_sb", [128, DC])
        bcol_bf = sbuf("bcol_bf", [128, DC], BF16)
        rl = sbuf("rl", [128, 1])
        lnt = {k: sbuf("ln_" + k, [128, 1]) for k in
               ["srow", "nmean", "t1", "sq2", "var", "std", "rstd", "bias2", "ssq", "epsv"]}
        lndump = sbuf("ln_dump", [128, D])
        qsc0 = {k: sbuf("qs_" + k, [128, 128]) for k in
                ["amax", "a2", "cf1", "cf2", "c", "p0", "p1", "p2", "pp", "m",
                 "f1", "f2", "rc", "scl", "inv"]}

        SLA_N = 25088
        SLB_N = 23552
        slabA = sbuf("slabA", [128, SLA_N])
        slabB = sbuf("slabB", [128, SLB_N])
        psum = ST.enter_context(nc.psum_tensor("ps", [128, 4096], F32))

        sems = {}
        for s in ALLSEMS:
            sems[s] = ST.enter_context(nc.semaphore(s))

        p = Prog()
        LA = Layout(slabA.ap(), SLA_N)
        LB = Layout(slabB.ap(), SLB_N)
        PS = psum.ap()
        PSbf = PS.bitcast(BF16)

        mg_tok = emit_magic(p, tiny)
        magic_ap = tiny["magic"][:]
        nmagic_ap = tiny["nmagic"][:]
        t_id = p.dma("sync", lambda e: e.dma_start(out=ident_bf[:], in_=ident_bf_d), "dls")
        t_mask = p.dma("sync", lambda e: e.dma_start(
            out=masks.ap().rearrange("p (r q) -> p r q", r=8), in_=masks_d.rearrange("r p q -> p r q")), "dls")
        oc = p.op("vector", lambda e: e.memset(onecol[:], 1.0), inc="sv")
        p.op("vector", lambda e: e.memset(lnt["epsv"][:], 1e-5), inc="sv")
        onecol_tok = oc

        LA.reset(); LB.reset()
        w_bf = {w: LA.bf16(DC * D) for w in ["wq", "wk", "wv"]}
        wdeps = {}
        for wname, dram in [("wq", wq_d), ("wk", wk_d), ("wv", wv_d)]:
            wdeps[wname] = p.dma("gpsimd", lambda e, w=wname, dr=dram: e.dma_start(
                out=w_bf[w].rearrange("p (dc k) -> p dc k", dc=DC),
                in_=dr.rearrange("(dc p) k -> p dc k", p=128)), "dlg")
        a_w32 = LA.f32(WT * D)
        a_wq = LA.bf16(WT * D)
        a_p32 = LA.f32(DC * WSH)
        a_pq = LA.bf16(DC * WSH)
        a_tmp = LB.f32(D)
        a_xbuf = LB.f32(D)
        a_stage = [LB.bf16(DC * 128) for _ in range(2)]

        t_w = p.dma("sync", lambda e: e.dma_start(
            out=a_w32.rearrange("p (t d) -> p t d", t=WT),
            in_=wfc_sh.rearrange("(t p) d -> p t d", p=128)), "dls")
        t_p = p.dma("sync", lambda e: e.dma_start(
            out=a_p32.rearrange("p (t w) -> p t w", t=DC),
            in_=wproj_sh.rearrange("(t p) w -> p t w", p=128)), "dls")

        NBW = D // 32
        NBP = WSH // 32
        am = []
        for i in range(WT):
            am.append(p.op("vector", lambda e, i=i: e.tensor_reduce(
                qsc0["amax"][:, i * NBW:(i + 1) * NBW],
                a_w32[:, i * D:(i + 1) * D].rearrange("p (b k) -> p b k", k=32),
                axis=AX, op=Op.max, apply_absolute_value=True),
                deps=[t_w, mg_tok], inc="sv"))
        tsc = emit_quant_scales_pre(p, WT * NBW, qsc0, magic_ap, am)
        tq_toks = []
        last_q = None
        for i in range(WT):
            tok = emit_quant_apply_x(p, a_w32[:, i * D:(i + 1) * D],
                                     a_wq[:, i * D:(i + 1) * D], i * NBW, NBW,
                                     qsc0, a_tmp, a_xbuf, magic_ap,
                                     tsc + ([last_q] if last_q else []))
            tq_toks.append(tok); last_q = tok
        am = []
        for i in range(DC):
            am.append(p.op("vector", lambda e, i=i: e.tensor_reduce(
                qsc0["amax"][:, i * NBP:(i + 1) * NBP],
                a_p32[:, i * WSH:(i + 1) * WSH].rearrange("p (b k) -> p b k", k=32),
                axis=AX, op=Op.max, apply_absolute_value=True),
                deps=[t_p, last_q], inc="sv"))
        tsc = emit_quant_scales_pre(p, DC * NBP, qsc0, magic_ap, am)
        tp_toks = []
        for i in range(DC):
            tok = emit_quant_apply_x(p, a_p32[:, i * WSH:(i + 1) * WSH],
                                     a_pq[:, i * WSH:(i + 1) * WSH], i * NBP, NBP,
                                     qsc0, a_tmp[:, 0:WSH], a_xbuf[:, 0:WSH],
                                     magic_ap, tsc + [last_q])
            tp_toks.append(tok); last_q = tok

        ev_war = None
        dma_war = {}
        for i in range(WT):
            sg = a_stage[i % 2]
            tts = []
            for dci in range(DC):
                deps = [tq_toks[i], t_id] + ([ev_war] if ev_war else [])
                tts.append(p.op("tensor", lambda e, i=i, dci=dci: e.transpose(
                    PSbf[:, (i % 2) * 1024 + dci * 128: (i % 2) * 1024 + (dci + 1) * 128],
                    a_wq[:, i * D + dci * 128: i * D + (dci + 1) * 128],
                    ident_bf[:]), deps=deps, inc="pe"))
            ev = p.op("scalar", lambda e, i=i, sg=sg: e.activation(
                sg[:, 0:DC * 128], PSbf[:, (i % 2) * 1024:(i % 2) * 1024 + DC * 128],
                AF.Identity),
                deps=[tts[-1]] + ([dma_war.get(i - 2)] if (i - 2) in dma_war else []),
                inc="ss")
            ev_war = ev
            dma_war[i] = p.dma("gpsimd", lambda e, i=i, sg=sg: e.dma_start(
                out=wfc_in_v.rearrange("(dc q) h -> q dc h", q=128)[:, :, i * 128:(i + 1) * 128],
                in_=sg.rearrange("p (dc t) -> p dc t", dc=DC)), "dlg", deps=[ev])
        for i in range(DC):
            j = WT + i
            sg = a_stage[j % 2]
            tts = []
            for hci in range(WT):
                deps = [tp_toks[i], t_id] + ([ev_war] if ev_war else [])
                tts.append(p.op("tensor", lambda e, i=i, hci=hci, j=j: e.transpose(
                    PSbf[:, (j % 2) * 1024 + hci * 128: (j % 2) * 1024 + (hci + 1) * 128],
                    a_pq[:, i * WSH + hci * 128: i * WSH + (hci + 1) * 128],
                    ident_bf[:]), deps=deps, inc="pe"))
            ev = p.op("scalar", lambda e, i=i, j=j, sg=sg: e.activation(
                sg[:, 0:WSH], PSbf[:, (j % 2) * 1024:(j % 2) * 1024 + WSH], AF.Identity),
                deps=[tts[-1]] + ([dma_war.get(j - 2)] if (j - 2) in dma_war else []),
                inc="ss")
            ev_war = ev
            dma_war[j] = p.dma("gpsimd", lambda e, i=i, sg=sg: e.dma_start(
                out=wpj_in_v.rearrange("(hc q) dd -> q hc dd", q=128)[:, :, i * 128:(i + 1) * 128],
                in_=sg[:, 0:WSH].rearrange("p (hc t) -> p hc t", hc=WT)), "dlg", deps=[ev])

        dlg_now = ("dlg", p.count["dlg"])

        bar = p.barrier()
        LA.reset(); LB.reset()
        LA.f32(3 * (DC * D // 2))
        WV = min(4, PT)
        xa = [LA.f32(D) for _ in range(WV)]
        xnf = [LA.f32(D) for _ in range(WV)]
        xnbf = [LA.bf16(D) for _ in range(WV)]
        xnq = [LB.bf16(D) for _ in range(WV)]
        xnT = LB.bf16(DC * PROD)
        stg = [LB.bf16(512) for _ in range(4)]
        stg2 = [LB.bf16(D) for _ in range(4)]
        q_tmp = LB.f32(D)
        q_xbuf = LB.f32(D)
        lndump2 = LB.f32(D)
        xnqt_stage = [LB.bf16(DC * 128) for _ in range(4)]
        xa2 = [LB.f32(D) for _ in range(WV)]
        lnw = [dict({k: sbuf(f"lw{i}_{k}", [128, WV]) for k in
                     ["srow", "nmean", "t1", "sq2", "var", "lnv", "rstd", "bias2", "ssq"]},
                    dump=lndump.ap(), dump2=lndump2, epsv=lnt["epsv"]) for i in range(2)]

        xp_t = x_p.rearrange("(t p) d -> t p d", p=128)
        NW = PT // WV
        WTOK = 128 * WV

        qtoks = {}
        pstate = {"stg_dma": {}, "stg2_dma": {}, "xq_dma": {}, "kcev": None,
                  "vev": None, "xqev": None}

        def emit_projchunk(wname, dest_cc, toff, ntok, dep_xnT):
            st_ = pstate
            for kc in range(DC):
                sg = stg[kc % 4]
                sem = f"ds{kc % 4}"
                bank = (kc % 2) * 512
                mms = []
                for dci in range(DC):
                    deps = [dep_xnT, wdeps[wname]] +                         ([st_["kcev"]] if st_["kcev"] else [])
                    mms.append(p.op("tensor", lambda e, kc=kc, dci=dci, bank=bank,
                                    toff=toff, ntok=ntok: e.matmul(
                        PS[:, bank:bank + ntok],
                        w_bf[wname][:, dci * D + kc * 128: dci * D + (kc + 1) * 128],
                        xnT[:, dci * PROD + toff: dci * PROD + toff + ntok],
                        start=(dci == 0), stop=(dci == DC - 1)),
                        deps=deps, inc="pe" if dci == DC - 1 else None))
                ev = p.op("scalar", lambda e, sg=sg, bank=bank, ntok=ntok: e.activation(
                    sg[:, 0:ntok], PS[:, bank:bank + ntok], AF.Identity),
                    deps=[mms[-1], st_["stg_dma"].get(kc % 4)], inc="ss")
                st_["kcev"] = ev
                st_["stg_dma"][kc % 4] = p.dma("gpsimd", lambda e, kc=kc, sg=sg,
                                               toff=toff, ntok=ntok: e.dma_start(
                    out=dest_cc[kc * 128:(kc + 1) * 128, toff:toff + ntok],
                    in_=sg[:, 0:ntok]), sem, deps=[ev])

        def emit_vtile(tt, dest_fn, dep_xnT):
            st_ = pstate
            sg = stg2[tt % 4]
            sem = f"dv{tt % 4}"
            base = 2048 + (tt % 2) * 1024
            mms = []
            for dci in range(DC):
                for h in range(2):
                    deps = [dep_xnT, wdeps["wv"]] + ([st_["vev"]] if st_["vev"] else [])
                    mms.append(p.op("tensor", lambda e, tt=tt, dci=dci, h=h, base=base: e.matmul(
                        PS[:, base + h * 512: base + (h + 1) * 512],
                        xnT[:, dci * PROD + tt * 128: dci * PROD + (tt + 1) * 128],
                        w_bf["wv"][:, dci * D + h * 512: dci * D + (h + 1) * 512],
                        start=(dci == 0), stop=(dci == DC - 1)),
                        deps=deps, inc="pe" if (dci == DC - 1 and h == 1) else None))
            ev = p.op("scalar", lambda e, sg=sg, base=base: e.activation(
                sg[:, 0:D], PS[:, base:base + D], AF.Identity),
                deps=[mms[-1], st_["stg2_dma"].get(tt % 4)], inc="ss")
            st_["vev"] = ev
            st_["stg2_dma"][tt % 4] = dest_fn(tt, sg, ev, sem)

        def emit_rows(which, entry_war=()):
            ln_prev = {}
            ev_war = None
            q_last_l = None
            for w in range(NW):
                tiles = [w * WV + i for i in range(WV)]
                st = lnw[w % 2]
                xw = xa if w % 2 == 0 else xa2
                ldeps = []
                for i, tt in enumerate(tiles):
                    if (w - 2) in ln_prev:
                        war = [ln_prev.get(w - 2)]
                    elif which == "prod":
                        war = [tq_toks[-1]] if w % 2 == 0 else []
                    else:
                        war = [prod_ln]
                    xsem = f"dx{w % 2}"
                    if which == "prod":
                        tok = p.dma("sync", lambda e, tt=tt, i=i, xw=xw: e.dma_start(
                            out=xw[i], in_=xp_t[tt]), xsem, deps=war)
                    else:
                        tok = p.dma("sync", lambda e, tt=tt, i=i, xw=xw: e.dma_start(
                            out=xw[i], in_=x_q[tt]), xsem, deps=war)
                    ldeps.append([tok])
                outs = [[xnbf[i]] if which == "prod" else [xnbf[i], xnf[i]]
                        for i in range(WV)]
                lt = emit_ln_wave(p, [xw[i] for i in range(WV)], outs, st, ldeps,
                                  war=([ln_prev.get(w - 1)] if (w - 1) in ln_prev
                                       else bar + list(entry_war)))
                ln_prev[w] = lt[-1]
                if which == "query":
                    am = []
                    NBX = D // 32
                    for i in range(WV):
                        am.append(p.op("vector", lambda e, i=i: e.tensor_reduce(
                            qsc0["amax"][:, i * NBX:(i + 1) * NBX],
                            xnf[i].rearrange("p (b k) -> p b k", k=32),
                            axis=AX, op=Op.max, apply_absolute_value=True),
                            deps=[lt[i], mg_tok] + ([q_last_l] if q_last_l else []),
                            inc="sv"))
                    tsc = emit_quant_scales_pre(p, WV * NBX, qsc0, magic_ap, am)
                    for i in range(WV):
                        q_last_l = emit_quant_apply_x(
                            p, xnf[i], xnq[i], i * NBX, NBX, qsc0, q_tmp,
                            q_xbuf, magic_ap,
                            tsc + ([q_last_l] if q_last_l else []))
                        qtoks[tiles[i]] = q_last_l
                for i, tt in enumerate(tiles):
                    tts = []
                    for dci in range(DC):
                        deps = [lt[i], t_id] + ([ev_war] if ev_war else []) +                             list(entry_war)
                        tts.append(p.op("tensor", lambda e, i=i, dci=dci: e.transpose(
                            PSbf[:, 2048 + dci * 128: 2048 + (dci + 1) * 128],
                            xnbf[i][:, dci * 128:(dci + 1) * 128], ident_bf[:]),
                            deps=deps, inc="pe"))
                    ev_war = p.op("scalar", lambda e, tt=tt: e.activation(
                        xnT.rearrange("p (dc t2) -> p dc t2", dc=DC)[:, :, tt * 128:(tt + 1) * 128],
                        PSbf[:, 2048:2048 + DC * 128].rearrange("p (dc t2) -> p dc t2", dc=DC),
                        AF.Identity), deps=[tts[-1]] + list(entry_war), inc="ss")
                    if which == "query":
                        sgx = xnqt_stage[tt % 4]
                        sem = f"dq{tt % 4}"
                        tts2 = []
                        for dci in range(DC):
                            deps = [qtoks[tt], t_id] +                                 ([pstate["xqev"]] if pstate["xqev"] else [])
                            tts2.append(p.op("tensor", lambda e, i=i, dci=dci: e.transpose(
                                PSbf[:, 3072 + dci * 128: 3072 + (dci + 1) * 128],
                                xnq[i][:, dci * 128:(dci + 1) * 128], ident_bf[:]),
                                deps=deps, inc="pe"))
                        ev2 = p.op("scalar", lambda e, sgx=sgx: e.activation(
                            sgx[:, 0:DC * 128], PSbf[:, 3072:3072 + DC * 128],
                            AF.Identity),
                            deps=[tts2[-1], pstate["xq_dma"].get(tt % 4)],
                            inc="ss")
                        pstate["xqev"] = ev2
                        pstate["xq_dma"][tt % 4] = p.dma(
                            "gpsimd", lambda e, tt=tt, sgx=sgx: e.dma_start(
                                out=sp_xnqt.ap()[tt], in_=sgx[:, 0:DC * 128]), sem,
                            deps=[ev2])
                xnT_wave = ev_war
                if which == "prod":
                    for i, tt in enumerate(tiles):
                        emit_vtile(tt, lambda tt, sg, ev, sem: p.dma(
                            "gpsimd", lambda e, tt=tt, sg=sg: e.dma_start(
                                out=v_in_v[tt * 128:(tt + 1) * 128, :],
                                in_=sg[:, 0:D]), sem, deps=[ev]), xnT_wave)
                    emit_projchunk("wk", kt_in_v, w * WTOK, WTOK, xnT_wave)
                else:
                    for i, tt in enumerate(tiles):
                        emit_vtile(tt, lambda tt, sg, ev, sem: p.dma(
                            "gpsimd", lambda e, tt=tt, sg=sg: e.dma_start(
                                out=sp_vq.ap()[tt], in_=sg[:, 0:D]), sem,
                            deps=[ev]), xnT_wave)
                    emit_projchunk("wq", sp_qt.ap(), w * WTOK, WTOK, xnT_wave)
            return ev_war, ln_prev[NW - 1]

        xnT_done, prod_ln = emit_rows("prod")
        ktv_sp = [(f"ds{i}", p.count[f"ds{i}"]) for i in range(4)] + \
                 [(f"dv{i}", p.count[f"dv{i}"]) for i in range(4)]
        t_ccw = p.op("gpsimd", lambda e: e.collective_compute(
            "AllGather", Op.bypass, replica_groups=[list(range(NCORES))],
            ins=[cc_wi.ap()], outs=[cc_wg.ap()]), deps=[dlg_now], inc="cc")
        t_ccp = t_ccw

        kv_pe = ("pe", p.count["pe"])
        _, _ = emit_rows("query", entry_war=[kv_pe, prod_ln])
        q_last = qtoks[NT - 1]
        xnqt_dma = [pstate["xq_dma"][i] for i in pstate["xq_dma"]]

        bar = p.barrier()
        LA.reset(); LB.reset()
        wfcqT_sb = LA.bf16(DC * HID)
        hb2 = [LA.f32(HID // 2) for _ in range(2)]
        ffn_tmp1 = LA.f32(HID // 2)
        bfc_row = LA.bf16(HID)[0:32, :]
        bproj_row = LA.bf16(D)[0:32, :]
        wprojqT_sb = LB.bf16(HC * D)
        hq = [LB.bf16(HID) for _ in range(2)]
        ffn_xbuf = LB.f32(HID // 2)
        hqT = hq
        xnqt_t = [LB.bf16(DC * 128) for _ in range(2)]

        wf4 = wfcqT_sb.rearrange("p (dc g h) -> p dc g h", dc=DC, g=NCORES)
        wp4 = wprojqT_sb.rearrange("p (g x dd) -> p g x dd", g=NCORES, x=WT)
        def load_xnqt(u, deps):
            pp = u % 2
            return p.dma("gpsimd", lambda e, u=u, pp=pp: e.dma_start(
                out=xnqt_t[pp][:, :], in_=sp_xnqt.ap()[u]), "dlg", deps=deps)
        xnqt_ld = [None, None]
        xnqt_ld[0] = load_xnqt(0, bar + [q_last])
        xnqt_ld[1] = load_xnqt(1, bar + [q_last])
        wf_toks = []
        for g in range(NCORES):
            wf_toks.append(p.dma("gpsimd", lambda e, g=g: e.dma_start(
                out=wf4[:, :, g, :],
                in_=wfc_g(g).rearrange("(dc p) h -> p dc h", p=128)), "dlg",
                deps=[t_ccw] + bar))
        for g in range(NCORES):
            t_wp = p.dma("sync", lambda e, g=g: e.dma_start(
                out=wp4[:, g, :, :],
                in_=wpj_g(g).rearrange("(x p) dd -> p x dd", p=128)), "dls",
                deps=[t_ccp] + bar)
        z1 = p.op("vector", lambda e: e.memset(bfc_row[:, :], 0.0), deps=bar, inc="sv")
        z2 = p.op("vector", lambda e: e.memset(bproj_row[:, :], 0.0), deps=bar, inc="sv")
        t_bf = p.dma("gpsimd", lambda e: e.dma_start(out=bfc_row[0:1, :], in_=bfc_r[0:1, :]),
                     "dlg", deps=bar + [z1, z2])
        t_bp = p.dma("gpsimd", lambda e: e.dma_start(out=bproj_row[0:1, :], in_=bproj_r[0:1, :]),
                     "dlg")
        z3 = p.op("vector", lambda e: e.memset(onesrow[:, :], 0.0), deps=bar, inc="sv")
        orow = p.op("vector", lambda e: e.memset(onesrow[0:1, :], 1.0), deps=[z3], inc="sv")
        t_cckv = p.op("gpsimd", lambda e: e.collective_compute(
            "AllGather", Op.bypass, replica_groups=[list(range(NCORES))],
            ins=[cc_kvi.ap()], outs=[cc_kvg.ap()]), deps=ktv_sp, inc="cc")
        t_cckt = t_ccv = t_cckv

        evac_toks = {}
        apply_half = {}
        apply_tok = {}
        hqT_tok = {}
        g_toks = []
        projread_tok = {}
        fc_pe_tok = {}

        FCBANK = (0, 512, 1536, 2048)

        def emit_fc(u):
            pp = u % 2
            deps0 = bar + [q_last, t_bf, orow, xnqt_ld[pp]]
            for h in range(2):
                k = 2 * u + h
                hbuf = hb2[k % 2]
                evs = []
                for g in range(4):
                    gg = 4 * h + g
                    bank = FCBANK[g]
                    war = [evac_toks[k - 1][g]] if (k - 1) in evac_toks else []
                    war = war + [wf_toks[gg]]
                    mms = []
                    for dci in range(DC):
                        mms.append(p.op("tensor", lambda e, gg=gg, dci=dci, bank=bank, pp=pp: e.matmul(
                            PS[:, bank:bank + 512],
                            xnqt_t[pp][:, dci * 128:(dci + 1) * 128],
                            wfcqT_sb[:, dci * HID + gg * 512: dci * HID + (gg + 1) * 512],
                            start=(dci == 0), stop=False),
                            deps=deps0 + war, inc=None))
                    mms.append(p.op("tensor", lambda e, gg=gg, bank=bank: e.matmul(
                        PS[:, bank:bank + 512], onesrow[:],
                        bfc_row[:, gg * 512:(gg + 1) * 512],
                        start=False, stop=True), deps=deps0 + war, inc="pe"))
                    hbwar = [apply_half[k - 2]] if (k - 2) in apply_half else []
                    evs.append(p.op("scalar", lambda e, g=g, bank=bank, hbuf=hbuf: e.activation(
                        hbuf[:, g * 512:(g + 1) * 512], PS[:, bank:bank + 512],
                        AF.Gelu), deps=[mms[-1]] + hbwar, inc="ss"))
                fc_pe_tok[u] = mms[-1]
                evac_toks[k] = evs

        def emit_mid(u):
            NBH = (HID // 2) // 32
            ams = []
            for h in range(2):
                k = 2 * u + h
                hbuf = hb2[k % 2]
                war = [apply_half[2 * u - 1]] if (2 * u - 1) in apply_half else []
                ams.append(p.op("vector", lambda e, hbuf=hbuf, h=h: e.tensor_reduce(
                    qsc0["amax"][:, h * NBH:(h + 1) * NBH],
                    hbuf.rearrange("p (b kk) -> p b kk", kk=32),
                    axis=AX, op=Op.max, apply_absolute_value=True),
                    deps=list(evac_toks[k]) + [mg_tok] + war, inc="sv"))
            tsc = emit_quant_scales_pre(p, 2 * NBH, qsc0, magic_ap, ams)
            war = []
            if (u - 2) in projread_tok:
                war.append(projread_tok[u - 2])
            prev = []
            for h in range(2):
                k = 2 * u + h
                hbuf = hb2[k % 2]
                apply_half[k] = emit_quant_apply_x(
                    p, hbuf, hq[u % 2][:, h * (HID // 2):(h + 1) * (HID // 2)],
                    h * NBH, NBH, qsc0, ffn_tmp1, ffn_xbuf, magic_ap,
                    tsc + war + prev)
                prev = [apply_half[k]]
            apply_tok[u] = apply_half[2 * u + 1]

        def emit_tail(u):
            hqu = hq[u % 2]
            ev_list = []
            for grp in range(HC // 8):
                tb = 2048 if grp % 2 == 0 else 5120
                tts = []
                for j in range(8):
                    hci = grp * 8 + j
                    deps = [apply_tok[u], t_id]
                    if grp >= 2:
                        deps.append(ev_list[grp - 2])
                    elif (u - 1) in hqT_tok:
                        deps.append(hqT_tok[u - 1])
                    elif (u - 2) in hqT_tok:
                        deps.append(hqT_tok[u - 2])
                    tts.append(p.op("tensor", lambda e, hci=hci, j=j, hqu=hqu, tb=tb: e.transpose(
                        PSbf[:, tb + j * 128: tb + (j + 1) * 128],
                        hqu[:, hci * 128:(hci + 1) * 128], ident_bf[:]),
                        deps=deps, inc="pe"))
                ev_list.append(p.op("scalar", lambda e, grp=grp, hqu=hqu, tb=tb: e.activation(
                    hqu[:, grp * 1024:(grp + 1) * 1024],
                    PSbf[:, tb:tb + 1024], AF.Identity), deps=[tts[-1]], inc="ss"))
            hqT_tok[u] = ev_list[-1]
            proj_mms = []
            war = [projread_tok[u - 1]] if (u - 1) in projread_tok else []
            for hci in range(HC):
                for h in range(2):
                    deps = [ev_list[-1], t_wp, t_bp] + war
                    proj_mms.append(p.op("tensor", lambda e, hci=hci, h=h, hqu=hqu: e.matmul(
                        PS[:, 3072 + h * 512: 3072 + (h + 1) * 512],
                        hqu[:, hci * 128:(hci + 1) * 128],
                        wprojqT_sb[:, hci * D + h * 512: hci * D + (h + 1) * 512],
                        start=(hci == 0), stop=False), deps=deps, inc=None))
            for h in range(2):
                proj_mms.append(p.op("tensor", lambda e, h=h: e.matmul(
                    PS[:, 3072 + h * 512: 3072 + (h + 1) * 512], onesrow[:],
                    bproj_row[:, h * 512:(h + 1) * 512],
                    start=False, stop=True), deps=[t_bp] + war,
                    inc="pe" if h == 1 else None))
            t_mm = proj_mms[-1]
            gout = xnqt_t[u % 2]
            t_g = p.op("vector", lambda e, gout=gout: e.tensor_copy(
                gout, PS[:, 3072:3072 + D]),
                deps=[t_mm, fc_pe_tok[u]], inc="sv")
            projread_tok[u] = t_g
            g_toks.append(p.dma("gpsimd", lambda e, u=u, gout=gout: e.dma_start(
                out=sp_g.ap()[u], in_=gout), "dlg", deps=[t_g]))

        for u in range(NT):
            pp = u % 2
            if u >= 2:
                war_x = [fc_pe_tok[u - 2]] if (u - 2) in fc_pe_tok else []
                if u - 4 >= 0:
                    war_x.append(g_toks[u - 4])
                xnqt_ld[pp] = load_xnqt(u, bar + [q_last] + war_x)
            emit_fc(u)
            if u >= 2:
                emit_tail(u - 2)
            emit_mid(u)
        emit_tail(NT - 2)
        emit_tail(NT - 1)
        t_g = g_toks[-1]
        p2_pe_end = ("pe", p.count["pe"])

        bar = p.barrier()
        LA.reset(); LB.reset()
        CKT = min(4, PROD // 128)
        CW = 128 * CKT
        CPM0 = 8 // CKT
        NBK = 2 * CPM0
        NBV = 2 * CPM0 - 1
        NCH = NKT // CKT
        QTs = LA.bf16(DC * PROD)
        acc = LA.f32(NT * 1025)
        ktb = [LB.bf16(DC * CW) for _ in range(NBK)]
        vtb = [LB.bf16(CKT * D) for _ in range(NBV)]
        expT = LB.bf16(8 * PROD)
        tmpb = LB.bf16(128)

        qt_sp = [(f"ds{i}", p.count[f"ds{i}"]) for i in range(4)]
        t_qt = p.dma("sync", lambda e: e.dma_start(
            out=QTs.rearrange("p (kc t) -> p kc t", kc=DC),
            in_=sp_qt.ap().rearrange("(kc p) t -> p kc t", p=128)), "dls",
            deps=qt_sp + [fc_pe_tok[NT - 1]])
        accz = p.op("vector", lambda e: e.memset(acc, 0.0), deps=bar, inc="sv")

        def load_k(q, deps):
            r = (CW * q) // PROD
            l0 = (CW * q) % PROD
            return p.dma("sync", lambda e, r=r, l0=l0, q=q: e.dma_start(
                out=ktb[q % NBK].rearrange("p (dc w) -> p dc w", dc=DC),
                in_=kt_g(r)[:, l0:l0 + CW].rearrange("(dc p) w -> p dc w", p=128)),
                f"dk{(q // CPM0) % 2}", deps=deps + [t_cckt])

        def load_v(q, deps):
            r = (CW * q) // PROD
            l0 = (CW * q) % PROD
            return p.dma("gpsimd", lambda e, r=r, l0=l0, q=q: e.dma_start(
                out=vtb[q % NBV].rearrange("p (s dd) -> p s dd", s=CKT),
                in_=v_g(r)[l0:l0 + CW, :].rearrange("(s p) dd -> p s dd", p=128)),
                f"dw{(q // CPM0) % 2}", deps=deps + [t_ccv])

        CPM = CPM0
        ktoks = {}
        vtoks = {}
        for q in range(min(2 * CPM, NCH)):
            ktoks[q] = load_k(q, [p2_pe_end])
        for q in range(min(NBV, NCH)):
            vtoks[q] = load_v(q, [p2_pe_end])
        chunk_pe = {}
        attv_pe = {}
        exp_toks = {}
        tm_prev = [None]
        merge_tok = [accz] * NT
        slot_war = [None, None]
        sc_war = [None, None]
        for ms in range(NMS):
            for kt in range(8 * ms, 8 * ms + 8):
                q = kt // CKT
                tk = ktoks[q]
                ub = kt // 8
                qoff = 128 * ub
                chunks = []
                pos = qoff
                while pos < PROD:
                    ln = min(512, PROD - pos)
                    chunks.append((pos, ln))
                    pos += ln
                for ci, (qs, ln) in enumerate(chunks):
                    bank = (ci % 2) * 512
                    mms = []
                    for dci in range(DC):
                        deps = [tk, t_qt] + ([sc_war[ci % 2]] if sc_war[ci % 2] else [])
                        mms.append(p.op(
                            "tensor",
                            lambda e, kt=kt, dci=dci, qs=qs, ln=ln, bank=bank: e.matmul(
                                PS[:, bank:bank + ln],
                                ktb[(kt // CKT) % NBK][:, dci * CW + (kt % CKT) * 128:
                                                       dci * CW + (kt % CKT) * 128 + 128],
                                QTs[:, dci * PROD + qs: dci * PROD + qs + ln],
                                start=(dci == 0), stop=(dci == DC - 1)),
                            deps=deps, inc="pe" if dci == DC - 1 else None))
                    chunk_pe[q] = mms[-1]
                    col0 = (kt % 8) * PROD + qs
                    if ci == 0:
                        t_e0 = p.op("scalar", lambda e, bank=bank: e.activation(
                            tmpb[:, 0:128], PS[:, bank:bank + 128], AF.Exp,
                            scale=1.0 / 32.0), deps=[mms[-1], tm_prev[0]], inc="ss")
                        t_m = p.op("vector", lambda e, kt=kt, col0=col0: e.tensor_tensor(
                            expT[:, col0:col0 + 128], tmpb[:, 0:128],
                            masks[:, (kt % 8) * 128:(kt % 8) * 128 + 128], op=Op.mult),
                            deps=[t_e0, t_mask], inc="sv")
                        tm_prev[0] = t_m
                        ek = [t_m]
                        if ln > 128:
                            t_e1 = p.op("scalar",
                                        lambda e, bank=bank, ln=ln, col0=col0: e.activation(
                                            expT[:, col0 + 128: col0 + ln],
                                            PS[:, bank + 128:bank + ln], AF.Exp,
                                            scale=1.0 / 32.0), deps=[mms[-1]], inc="ss")
                            ek.append(t_e1)
                            sc_war[ci % 2] = t_e1
                        else:
                            sc_war[ci % 2] = t_e0
                        exp_toks[kt] = ek
                    else:
                        t_e = p.op("scalar",
                                   lambda e, bank=bank, ln=ln, col0=col0: e.activation(
                                       expT[:, col0: col0 + ln], PS[:, bank:bank + ln],
                                       AF.Exp, scale=1.0 / 32.0), deps=[mms[-1]], inc="ss")
                        sc_war[ci % 2] = t_e
                        exp_toks[kt].append(t_e)
            for q in range(CPM * (ms + 2), CPM * (ms + 3)):
                if q < NCH and q not in ktoks:
                    war = [chunk_pe.get(q - NBK)]
                    ktoks[q] = load_k(q, [w for w in war if w])
            for u in range(ms, NT):
                sl = u % 2
                base = 1024 + sl * 1536
                first = True
                t_mm = None
                for kt in range(8 * ms, 8 * ms + 8):
                    deps = list(exp_toks[kt]) + [vtoks[kt // CKT]]
                    if slot_war[sl] is not None:
                        deps.append(slot_war[sl])
                    ecol = (kt % 8) * PROD + u * 128
                    vt = vtb[(kt // CKT) % NBV]
                    for part in range(2):
                        p.op("tensor",
                             lambda e, kt=kt, part=part, base=base, ecol=ecol, vt=vt, first=first, ms=ms: e.matmul(
                                 PS[:, base + part * 512: base + (part + 1) * 512],
                                 expT[:, ecol:ecol + 128],
                                 vt[:, (kt % CKT) * D + part * 512:(kt % CKT) * D + (part + 1) * 512],
                                 start=first, stop=(kt == 8 * ms + 7)),
                             deps=deps, inc=None)
                    t_mm = p.op("tensor",
                                lambda e, kt=kt, base=base, ecol=ecol, first=first, ms=ms: e.matmul(
                                    PS[:, base + 1024: base + 1025],
                                    expT[:, ecol:ecol + 128], onecol[:],
                                    start=first, stop=(kt == 8 * ms + 7)),
                                deps=deps + [onecol_tok],
                                inc="pe" if kt == 8 * ms + 7 else None)
                    first = False
                mtk = p.op("vector", lambda e, u=u, base=base: e.tensor_tensor(
                    acc[:, u * 1025:(u + 1) * 1025], acc[:, u * 1025:(u + 1) * 1025],
                    PS[:, base:base + 1025], op=Op.add),
                    deps=[t_mm, merge_tok[u]], inc="sv")
                merge_tok[u] = mtk
                slot_war[sl] = mtk
                for q in range(CPM * ms, CPM * (ms + 1)):
                    attv_pe[q] = t_mm
            for q in range(CPM * (ms + 1), CPM * (ms + 3)):
                if q < NCH and q not in vtoks:
                    if q - NBV >= 0 and attv_pe.get(q - NBV) is None:
                        continue
                    war = [attv_pe.get(q - NBV)]
                    vtoks[q] = load_v(q, [w for w in war if w])


        bar = p.barrier()
        LB.reset()
        W4 = min(4, NT)
        wfca_sb = LB.bf16(DC * D)
        bfca2_row = LB.bf16(D)[0:32, :]
        t4A = [LB.f32(D) for _ in range(W4)]
        bfrow_f32 = t4A[0][0:1, :]
        t4B = [LB.f32(D) for _ in range(W4)]
        xt4 = [LB.f32(D) for _ in range(W4)]
        gt = [LB.bf16(D) for _ in range(W4)]
        vt4 = [LB.bf16(D) for _ in range(W4)]
        zt = vt4
        zT = [LB.bf16(DC * 128) for _ in range(W4)]
        rl4 = sbuf("rl4", [128, W4])
        lnw4 = [dict(d, dump2=None) for d in lnw]

        t_wfca = p.dma("gpsimd", lambda e: e.dma_start(
            out=wfca_sb.rearrange("p (dc k) -> p dc k", dc=DC),
            in_=wfca_d.rearrange("(dc p) k -> p dc k", p=128)), "dlg", deps=bar)
        t_gc = p.dma("sync", lambda e: e.dma_start(out=gcol_sb[:], in_=gamma_col), "dls",
                     deps=bar)
        t_bc0 = p.dma("sync", lambda e: e.dma_start(out=bcol_sb[:], in_=beta_col), "dls")
        t_bc = p.op("vector", lambda e: e.tensor_copy(bcol_bf[:], bcol_sb[:]),
                    deps=[t_bc0] + bar, inc="sv")
        t_bfr = p.dma("sync", lambda e: e.dma_start(
            out=bfca2_row[0:1, :].bitcast(F32), in_=bfca_r[0:1, :].bitcast(BF16) if False
            else bfca_r[0:1, :]), "dls") if False else None
        bmm = []
        for dci in range(DC):
            for h in range(2):
                bmm.append(p.op("tensor", lambda e, dci=dci, h=h: e.matmul(
                    PS[0:1, 3072 + h * 512: 3072 + (h + 1) * 512],
                    bcol_bf[:, dci:dci + 1],
                    wfca_sb[:, dci * D + h * 512: dci * D + (h + 1) * 512],
                    start=(dci == 0), stop=(dci == DC - 1)),
                    deps=[t_wfca, t_bc] + bar,
                    inc="pe" if (dci == DC - 1 and h == 1) else None))
        zb = p.op("vector", lambda e: e.memset(bfca2_row[:, :], 0.0), deps=bar, inc="sv")
        t_bfca = p.dma("sync", lambda e: e.dma_start(
            out=bfrow_f32[0:1, :], in_=bfca_r[0:1, :]), "dls", deps=bar)
        b2 = p.op("vector", lambda e: e.scalar_tensor_tensor(
            bfca2_row[0:1, :], PS[0:1, 3072:3072 + D], 1.0, bfrow_f32[0:1, :],
            op0=Op.mult, op1=Op.add), deps=[bmm[-1], t_bfca, zb], inc="sv")
        gsc = []
        for dci in range(DC):
            gsc.append(p.op("vector", lambda e, dci=dci: e.tensor_scalar(
                wfca_sb[:, dci * D:(dci + 1) * D], wfca_sb[:, dci * D:(dci + 1) * D],
                gcol_sb[:, dci:dci + 1], None, op0=Op.mult),
                deps=[b2, t_gc], inc="sv"))
        wfcg = gsc[-1]

        p4_hist = {}
        a7_hist = {}
        evz_hist = {}
        out_total = 0
        NW4 = NT // W4
        for w in range(NW4):
            us = [w * W4 + i for i in range(W4)]
            st = lnw4[w % 2]
            ldg = {}
            ldx = {}
            for i, u in enumerate(us):
                war = [p4_hist.get((w - 1, i))] if (w - 1, i) in p4_hist else bar
                ldg[i] = p.dma("gpsimd", lambda e, u=u, i=i: e.dma_start(
                    out=gt[i][:, :], in_=sp_g.ap()[u]), "dlg", deps=war)
                p.dma("gpsimd", lambda e, u=u, i=i: e.dma_start(
                    out=vt4[i][:, :], in_=sp_vq.ap()[u]), "dlg")
                ldx[i] = p.dma("sync", lambda e, u=u, i=i: e.dma_start(
                    out=xt4[i], in_=x_q[u]), "dls", deps=war)
            sls = []
            for i, u in enumerate(us):
                sls.append(p.op("scalar", lambda e, i=i: e.activation(
                    gt[i][:, :], gt[i][:, :], AF.Silu), deps=[ldg[i]], inc="ss"))
            a1s = []
            for i, u in enumerate(us):
                war = [p4_hist.get((w - 1, i))] if (w - 1, i) in p4_hist else bar
                a1s.append(p.op("vector", lambda e, u=u, i=i: e.reciprocal(
                    rl4[:, i:i + 1], acc[:, u * 1025 + 1024: u * 1025 + 1025]),
                    deps=[merge_tok[u]] + war, inc="sv"))
            a2s = []
            for i, u in enumerate(us):
                dep_extra = [b2] if (w == 0 and i == 0) else []
                if w > 0:
                    dep_extra.append(("dout", p.count["dout"]))
                a2s.append(p.op("scalar", lambda e, u=u, i=i: e.activation(
                    t4A[i], acc[:, u * 1025: u * 1025 + 1024], AF.Identity,
                    scale=rl4[:, i:i + 1]), deps=[a1s[i]] + dep_extra, inc="ss"))
            a3s = []
            for i, u in enumerate(us):
                a3s.append(p.op("vector", lambda e, i=i: e.tensor_tensor(
                    t4B[i], t4A[i], vt4[i][:, :], op=Op.add),
                    deps=[a2s[i], ldg[i]], inc="sv"))
            zn_toks = emit_ln_wave(p, [t4B[i] for i in range(W4)],
                                   [[zt[i]] for i in range(W4)], st,
                                   [[a3s[i]] for i in range(W4)],
                                   war=[p4_hist.get((w - 1, W4 - 1))] if w else bar)
            for i, u in enumerate(us):
                tts = []
                for dci in range(DC):
                    deps = [zn_toks[i], t_id, p4_hist.get((w - 1, i)),
                            evz_hist.get(i % 2)]
                    tts.append(p.op("tensor", lambda e, i=i, dci=dci: e.transpose(
                        PSbf[:, (i % 2) * 1024 + dci * 128:
                             (i % 2) * 1024 + (dci + 1) * 128],
                        zt[i][:, dci * 128:(dci + 1) * 128], ident_bf[:]),
                        deps=deps, inc="pe"))
                evz = p.op("scalar", lambda e, i=i: e.activation(
                    zT[i][:, 0:DC * 128],
                    PSbf[:, (i % 2) * 1024:(i % 2) * 1024 + DC * 128], AF.Identity),
                    deps=[tts[-1]], inc="ss")
                evz_hist[i % 2] = evz
                base = 1024 + (i % 2) * 1024
                mms = []
                for dci in range(DC):
                    for h in range(2):
                        mms.append(p.op("tensor", lambda e, i=i, dci=dci, h=h, base=base: e.matmul(
                            PS[:, base + h * 512: base + (h + 1) * 512],
                            zT[i][:, dci * 128:(dci + 1) * 128],
                            wfca_sb[:, dci * D + h * 512: dci * D + (h + 1) * 512],
                            start=(dci == 0), stop=False),
                            deps=[evz, wfcg, a7_hist.get(i % 2)], inc=None))
                for h in range(2):
                    mms.append(p.op("tensor", lambda e, h=h, base=base: e.matmul(
                        PS[:, base + h * 512: base + (h + 1) * 512], onesrow[:],
                        bfca2_row[:, h * 512:(h + 1) * 512],
                        start=False, stop=True), deps=[b2, orow],
                        inc="pe" if h == 1 else None))
                a7 = p.op("vector", lambda e, i=i, base=base: e.scalar_tensor_tensor(
                    t4A[i], PS[:, base:base + D], 1.0, gt[i][:, :],
                    op0=Op.mult, op1=Op.mult), deps=[mms[-1], sls[i]], inc="sv")
                a7_hist[i % 2] = a7
                a8 = p.op("vector", lambda e, i=i: e.tensor_tensor(
                    t4A[i], t4A[i], xt4[i], op=Op.add), deps=[a7, ldx[i]], inc="sv")
                p.dma("sync", lambda e, u=u, i=i: e.dma_start(
                    out=out_d[u], in_=t4A[i]), "dout", deps=[a8])
                out_total += 16
                p4_hist[(w, i)] = a8

        p.op("gpsimd", lambda e: e.nop(), deps=[("dout", out_total)])

        block = ST.enter_context(nc.Block())
        p.replay(block, sems)

    return nc



def make_masks(c):
    import ml_dtypes
    j = np.arange(128)[:, None]
    i = np.arange(128)[None, :]
    m = np.zeros((8, 128, 128), dtype=np.float32)
    for r in range(8):
        m[r] = ((j - i) <= 128 * (c - r)).astype(np.float32)
    return m.astype(ml_dtypes.bfloat16)


def kernel(L0_attn_x, L0_attn_Wq, L0_attn_Wk, L0_attn_Wv, L0_attn_Wfc, L0_attn_bfc,
           L0_attn_gamma, L0_attn_beta, L0_ffn_w_fc, L0_ffn_b_fc, L0_ffn_w_proj,
           L0_ffn_b_proj):
    import ml_dtypes
    x = np.ascontiguousarray(np.asarray(L0_attn_x, dtype=np.float32)[0])
    seq = x.shape[0]
    NT = seq // 128 // NCORES
    PROD = seq // NCORES
    WSH = HID // NCORES
    xt = x.reshape(NT, NCORES, 128, D)
    wfc = np.ascontiguousarray(np.asarray(L0_ffn_w_fc, dtype=np.float32))
    wproj = np.ascontiguousarray(np.asarray(L0_ffn_w_proj, dtype=np.float32))

    def rep(v):
        v = np.asarray(v, dtype=np.float32)
        return np.ascontiguousarray(np.broadcast_to(v[None, :], (128, v.shape[0])))

    ident_bf = np.eye(128, dtype=np.float32).astype(ml_dtypes.bfloat16)
    nc = build_nc(seq)
    in_maps = []
    for c in range(NCORES):
        in_maps.append({
            "x_q": np.ascontiguousarray(xt[:, c]),
            "x_p": np.ascontiguousarray(x[PROD * c: PROD * (c + 1)]),
            "wq": np.ascontiguousarray(np.asarray(L0_attn_Wq, dtype=np.float32)),
            "wk": np.ascontiguousarray(np.asarray(L0_attn_Wk, dtype=np.float32)),
            "wv": np.ascontiguousarray(np.asarray(L0_attn_Wv, dtype=np.float32)),
            "wfca": np.ascontiguousarray(np.asarray(L0_attn_Wfc, dtype=np.float32)),
            "wfc_sh": np.ascontiguousarray(wfc[WSH * c: WSH * (c + 1), :]),
            "wproj_sh": np.ascontiguousarray(wproj[:, WSH * c: WSH * (c + 1)]),
            "bfc_r": rep(L0_ffn_b_fc),
            "bproj_r": rep(L0_ffn_b_proj),
            "bfca_r": rep(L0_attn_bfc),
            "gamma_r": rep(L0_attn_gamma),
            "beta_r": rep(L0_attn_beta),
            "ident_bf": ident_bf,
            "masks": make_masks(c),
            "gamma_col": np.ascontiguousarray(
                np.asarray(L0_attn_gamma, dtype=np.float32).reshape(DC, 128).T),
            "beta_col": np.ascontiguousarray(
                np.asarray(L0_attn_beta, dtype=np.float32).reshape(DC, 128).T),
        })
    res = run_bass_kernel_spmd(nc, in_maps, core_ids=list(range(NCORES)))
    out = np.empty((NT, NCORES, 128, D), dtype=np.float32)
    for c in range(NCORES):
        out[:, c] = res.results[c]["out"]
    return out.reshape(1, seq, D)
